# version 02
# baseline (speedup 1.0000x reference)
import numpy as np
from contextlib import ExitStack

import concourse.bass as bass
import concourse.mybir as mybir
from concourse.bass_utils import run_bass_kernel_spmd

F32 = mybir.dt.float32
BF16 = mybir.dt.bfloat16
AF = mybir.ActivationFunctionType
ALU = mybir.AluOpType
AX = mybir.AxisListType

MAGIC = float(np.float32(1.5 * 2 ** 23))
NCORES = 8
NTOK = 4096
C = 1024
NPT = 8
NSLAB = 8
SLAB = 512
NTC = 32
I_FFN = 2816
NIC = 22
QMAX = 127.0



class Prog:
    ENGS = ("t", "v", "s", "g", "q")
    MS = {"t": "mt", "v": "mv", "s": "ms", "g": "mg"}
    DS = {"q": "dq", "g": "dg", "s": "ds"}

    def __init__(self):
        self.qs = {e: [] for e in self.ENGS}
        self.counts = {m: 0 for m in ("mt", "mv", "ms", "mg", "dq", "dg", "ds")}

    def op(self, e, fn, waits=(), inc=False):
        import inspect
        lineno = inspect.currentframe().f_back.f_lineno
        cnt = None
        sem = None
        if inc:
            sem = self.MS[e]
            self.counts[sem] += 1
            cnt = self.counts[sem]
        self.qs[e].append((fn, tuple(waits), sem, 1, lineno))
        return cnt

    def dma(self, e, out, in_, waits=(), **kw):
        import inspect
        lineno = inspect.currentframe().f_back.f_lineno
        sem = self.DS[e]
        self.counts[sem] += 16
        self.qs[e].append(
            (lambda eng, o=out, i=in_, k=kw: eng.dma_start(out=o, in_=i, **k),
             tuple(waits), sem, 16, lineno))
        return self.counts[sem]

    def dmat(self, e, out, in_, waits=()):
        import inspect
        lineno = inspect.currentframe().f_back.f_lineno
        sem = self.DS[e]
        self.counts[sem] += 16
        self.qs[e].append(
            (lambda eng, o=out, i=in_: eng.dma_start_transpose(o, i),
             tuple(waits), sem, 16, lineno))
        return self.counts[sem]

    def wait(self, e, waits):
        import inspect
        lineno = inspect.currentframe().f_back.f_lineno
        self.qs[e].append((None, tuple(waits), None, 0, lineno))

    def check(self):
        ptr = {e: 0 for e in self.ENGS}
        sems = {m: 0 for m in self.counts}
        total = sum(len(q) for q in self.qs.values())
        done = 0
        progress = True
        while progress:
            progress = False
            for e in self.ENGS:
                while ptr[e] < len(self.qs[e]):
                    fn, waits, sem, n, ln = self.qs[e][ptr[e]]
                    if any(sems[w] < v for (w, v) in waits):
                        break
                    if sem is not None:
                        sems[sem] += n
                    ptr[e] += 1
                    done += 1
                    progress = True
        if done != total:
            msg = [f"DEADLOCK: {done}/{total} ops executed"]
            for e in self.ENGS:
                if ptr[e] < len(self.qs[e]):
                    fn, waits, sem, n, ln = self.qs[e][ptr[e]]
                    blocked = [(w, v, sems[w]) for (w, v) in waits if sems[w] < v]
                    msg.append(f"  engine {e} op#{ptr[e]} line {ln}: needs {blocked}")
            raise RuntimeError("\n".join(msg))

    def realize(self, nc, es):
        sems = {m: es.enter_context(nc.semaphore(m))
                for m in ("mt", "mv", "ms", "mg", "dq", "dg", "ds")}
        block = es.enter_context(nc.Block())
        attr = {"t": "tensor", "v": "vector", "s": "scalar",
                "g": "gpsimd", "q": "sync"}
        for e in self.ENGS:
            ops = self.qs[e]

            def body(eng, ops=ops):
                for fn, waits, sem, n, ln in ops:
                    for (w, v) in waits:
                        eng.wait_ge(sems[w], v)
                    if fn is not None:
                        ins = fn(eng)
                        if sem is not None:
                            ins.then_inc(sems[sem], n)
            getattr(block, attr[e])(body)


def _ap2(t, row0, nrows, col0, ncols, rowstride):
    return bass.AP(t.tensor if isinstance(t, bass.AP) else t,
                   row0 * rowstride + col0, [[rowstride, nrows], [1, ncols]])


def _bcast_row(t, col0, ncols, nrows=128):
    return bass.AP(t.tensor if isinstance(t, bass.AP) else t,
                   col0, [[0, nrows], [1, ncols]])



def emit_ln_rowmath(p, rows, co, pst, mm_wait, epscol, extra_waits=()):
    def fmath(V, co=co):
        m = rows[0:1, co:co + SLAB]
        var = rows[0:1, co + SLAB:co + 2 * SLAB]
        minv = rows[0:1, co + 3 * SLAB:co + 4 * SLAB]
        V.tensor_scalar_mul(m, pst[0:1, 0:SLAB], 1.0 / C)
        V.tensor_scalar_mul(var, pst[0:1, SLAB:2 * SLAB], 1.0 / C)
        V.tensor_tensor(minv, m, m, ALU.mult)
        return V.tensor_tensor(var, var, minv, ALU.subtract)
    vc = p.op("v", fmath, waits=[("mt", mm_wait)] + list(extra_waits), inc=True)
    sc = p.op("s", lambda S, co=co: S.activation(
        rows[0:1, co + 2 * SLAB:co + 3 * SLAB],
        rows[0:1, co + SLAB:co + 2 * SLAB],
        AF.Sqrt, bias=epscol[0:1, 0:1]), waits=[("mv", vc)], inc=True)

    vr = p.op("v", lambda V, co=co: V.reciprocal(
        rows[0:1, co + 2 * SLAB:co + 3 * SLAB],
        rows[0:1, co + 2 * SLAB:co + 3 * SLAB]), waits=[("ms", sc)], inc=True)
    vc2 = p.op("v", lambda V, co=co: V.tensor_tensor(
        rows[0:1, co + 3 * SLAB:co + 4 * SLAB], rows[0:1, co:co + SLAB],
        rows[0:1, co + 2 * SLAB:co + 3 * SLAB], ALU.mult),
        waits=[("mv", vr)], inc=True)
    return vc, vc2


def emit_locmax_finish(p, bufs, mxacc, mxd_dram, out_ap, waits):
    d1 = p.dma("g", bass.AP(mxd_dram.tensor, 0, [[1, 128], [1, 1]]),
               mxacc[:], waits=waits)
    d2 = p.dma("g", bufs["mxrow"][:],
               bass.AP(mxd_dram.tensor, 0, [[128, 1], [1, 128]]),
               waits=[("dg", d1)])
    vc = p.op("v", lambda V: V.reduce_max(bufs["gmx"][:], bufs["mxrow"][:], AX.X),
              waits=[("dg", d2)], inc=True)
    return p.dma("q", out_ap, bufs["gmx"][:], waits=[("mv", vc)])



def build_L1(debug=False):
    nc = bass.Bass("TRN2", target_bir_lowering=False)
    x_in = nc.dram_tensor("x", [C, NTOK], F32, kind="ExternalInput").ap()
    qkvw_in = nc.dram_tensor("qkv_w", [C, 3 * C], F32, kind="ExternalInput").ap()
    qkvb_in = nc.dram_tensor("qkv_b", [1, 3 * C], F32, kind="ExternalInput").ap()
    projw_in = nc.dram_tensor("proj_w", [C, C], F32, kind="ExternalInput").ap()
    projb_in = nc.dram_tensor("proj_b", [1, C], F32, kind="ExternalInput").ap()
    x2_out = nc.dram_tensor("x2", [C, NTOK], F32, kind="ExternalOutput").ap()
    stat2_out = nc.dram_tensor("stat2", [2, NTOK], F32, kind="ExternalOutput").ap()
    lm_out = nc.dram_tensor("locmax", [1, 1], F32, kind="ExternalOutput").ap()
    kd = "ExternalOutput" if debug else "Internal"
    q_dram = nc.dram_tensor("q_spill", [NTOK, C], BF16, kind=kd).ap()
    stat1_dram = nc.dram_tensor("stat1_dram", [2, NTOK], F32, kind=kd).ap()
    stat2_dram = nc.dram_tensor("stat2_dram", [2, NTOK], F32).ap()
    if debug:
        kvw_dbg = nc.dram_tensor("kvw_dbg", [128, 2 * NPT * 128], BF16, kind=kd).ap()
        aot_dbg = nc.dram_tensor("aot_dbg", [C, NTOK], BF16, kind=kd).ap()
        mx_dbg = nc.dram_tensor("mx_dbg", [128, 1], F32, kind=kd).ap()
        rep_dbg = nc.dram_tensor("rep_dbg", [128, 2 * SLAB], F32, kind=kd).ap()
        tmax_dbg = nc.dram_tensor("tmax_dbg", [128, 2 * SLAB], F32, kind=kd).ap()

    es = ExitStack()
    sb = lambda n, shp, dt=F32: es.enter_context(nc.sbuf_tensor(n, shp, dt))
    psf = lambda n, shp: es.enter_context(nc.psum_tensor(n, shp, F32))
    bufs = {}
    xbf = sb("xbf", [128, 2 * NPT * SLAB], BF16)
    xln1 = sb("xln1", [128, 2 * NPT * 128], BF16)
    qkvw = sb("qkvw", [128, NPT * 3 * C], BF16)
    projw = sb("projw", [128, NPT * C], BF16)
    qkvb = sb("qkvb", [1, 3 * C], BF16)
    projb = sb("projb", [128, NPT], F32)
    onesf = sb("onesf", [1, 128], BF16)
    bufs["ones_bf"] = ones_bf = sb("ones_bf", [128, 1], BF16)
    bufs["identf"] = sb("identf", [128, 128], F32)
    epscol = sb("epscol", [128, 1], F32)
    bufs["gmx"] = sb("gmx", [1, 1], F32)
    mxacc = sb("mxacc", [128, 1], F32)
    mxt = sb("mxt", [128, 1], F32)
    sq_tile = sb("sq_tile", [128, 2 * SLAB], BF16)
    rows = sb("rows", [1, 4 * SLAB], F32)
    rep = sb("rep", [128, 2 * 2 * SLAB], F32)
    qkvbuf = sb("qkvbuf", [128, 2 * 2048], F32)
    sqq = sb("sqq", [128, 2048], BF16)
    l2s = sb("l2s", [128, 3 * 32], F32)
    qkn = sb("qkn", [128, 2 * 2048], BF16)
    vbf = sb("vbf", [128, 2 * C], BF16)
    kvw = sb("kvw", [128, 2 * NPT * 128], BF16)
    qT = sb("qT", [128, NPT * SLAB], BF16)
    aoT = sb("aoT", [128, 2 * NPT * SLAB], BF16)
    xold = sb("xold", [128, 2 * SLAB], F32)
    xout = sb("xout", [128, 2 * SLAB], F32)
    x2bf = sb("x2bf", [128, 2 * NPT * SLAB], BF16)
    x2sq = sb("x2sq", [128, 2 * SLAB], BF16)
    tmax = sb("tmax", [128, 2 * SLAB], F32)

    pst = psf("pst", [128, 1024])
    pmm = psf("pmm", [128, 2 * SLAB])
    pkv = psf("pkv", [128, NPT * 128])
    mxd_dram = nc.dram_tensor("mxd_dram", [128, 1], F32).ap()
    bufs["mxrow"] = sb("mxrow", [1, 128], F32)

    p = Prog()

    def fsetup(g):
        g.memset(onesf[:], 1.0)
        g.memset(ones_bf[:], 1.0)
        g.memset(kvw[:], 0.0)
        g.memset(mxacc[:], 0.0)
        g.memset(epscol[:], 1e-5)
        g.memset(bufs["identf"][:], 0.0)
        return g.affine_select(out=bufs["identf"][:], in_=bufs["identf"][:],
                               compare_op=ALU.not_equal, fill=1.0, base=0,
                               pattern=[[-1, 128]], channel_multiplier=1)
    g0 = p.op("g", fsetup, inc=True)

    d_w = p.dma("g", qkvw[:].rearrange("p (c n) -> p c n", c=NPT),
                qkvw_in.rearrange("(c p) n -> p c n", p=128))
    d_pw = p.dma("g", projw[:].rearrange("p (c n) -> p c n", c=NPT),
                 projw_in.rearrange("(c p) n -> p c n", p=128))
    d_b = p.dma("g", qkvb[:], qkvb_in)
    d_pb = p.dma("g", projb[:],
                 bass.AP(projb_in.tensor, 0, [[1, 128], [128, NPT]]),
                 allow_slow_non_contiguous=True)

    cons_ms = [0] * 2
    kv_ms = [0] * 2
    pmm_free = [0] * 2
    xln_used = [0] * 2
    rep_used = [0] * 2
    xbf_used = [0] * 2
    qspill_war = [0] * 2
    prev_vc = 0
    rowdma_prev = 0
    sq_war = [0, 0]
    sqq_war = 0
    kv_ev = None
    last_kv = None
    q_spill_dma = None
    for s in range(NSLAB):
        sp = s % 2
        wx = [("mg", g0)] if s == 0 else []
        if xbf_used[sp]:
            wx.append(("mv", xbf_used[sp]))
        d_x = p.dma("g", bass.AP(xbf, sp * NPT * SLAB,
                                 [[2 * NPT * SLAB, 128], [SLAB, NPT], [1, SLAB]]),
                    bass.AP(x_in.tensor, s * SLAB,
                            [[NTOK, 128], [128 * NTOK, NPT], [1, SLAB]]),
                    waits=wx)
        mm_cnt = None
        for pt in range(NPT):
            par = pt % 2
            w = [("dg", d_x)] if pt == 0 else []
            if sq_war[par]:
                w.append(("mt", sq_war[par]))
            cs = p.op("s", lambda S, pt=pt, sp=sp, par=par: S.activation(
                sq_tile[:, par * SLAB:(par + 1) * SLAB],
                xbf[:, sp * NPT * SLAB + pt * SLAB: sp * NPT * SLAB + (pt + 1) * SLAB],
                AF.Square), waits=w, inc=True)
            wmm = [("ms", cs)]
            if pt == 0 and prev_vc:
                wmm.append(("mv", prev_vc))
            p.op("t", lambda T, pt=pt, sp=sp: T.matmul(
                pst[0:1, 0:SLAB], ones_bf[:],
                xbf[:, sp * NPT * SLAB + pt * SLAB: sp * NPT * SLAB + (pt + 1) * SLAB],
                start=(pt == 0), stop=(pt == NPT - 1)), waits=wmm)
            mm_cnt = p.op("t", lambda T, par=par, pt=pt: T.matmul(
                pst[0:1, SLAB:2 * SLAB], ones_bf[:],
                sq_tile[:, par * SLAB:(par + 1) * SLAB],
                start=(pt == 0), stop=(pt == NPT - 1)), inc=True)
            sq_war[par] = mm_cnt
        co = 0
        wrm = [("dg", rowdma_prev)] if rowdma_prev else []
        prev_vc, vc2 = emit_ln_rowmath(p, rows, co, pst, mm_cnt, epscol, extra_waits=wrm)
        p.dma("g", _ap2(stat1_dram, 0, 1, s * SLAB, SLAB, NTOK),
              rows[0:1, co + 2 * SLAB:co + 3 * SLAB], waits=[("mv", vc2)])
        d_rows = p.dma("g", _ap2(stat1_dram, 1, 1, s * SLAB, SLAB, NTOK),
                       rows[0:1, co + 3 * SLAB:co + 4 * SLAB])
        rowdma_prev = d_rows
        w = [("dg", d_rows)]
        if rep_used[sp]:
            w.append(("mv", rep_used[sp]))
        p.dma("g", rep[:, sp * 2 * SLAB: sp * 2 * SLAB + SLAB],
              _bcast_row(stat1_dram, 0 * NTOK + s * SLAB, SLAB), waits=w)
        d_rep2 = p.dma("g", rep[:, sp * 2 * SLAB + SLAB: (sp + 1) * 2 * SLAB],
                       _bcast_row(stat1_dram, 1 * NTOK + s * SLAB, SLAB))
        for tci in range(4):
            tc = s * 4 + tci
            cp = tc % 2
            wv = [("dg", d_rep2)]
            if xln_used[cp]:
                wv.append(("mt", xln_used[cp]))

            def fxln(V, cp=cp, sp=sp, tci=tci):
                last = None
                for pt in range(NPT):
                    dst = xln1[:, cp * NPT * 128 + pt * 128: cp * NPT * 128 + (pt + 1) * 128]
                    src = xbf[:, sp * NPT * SLAB + pt * SLAB + tci * 128: sp * NPT * SLAB + pt * SLAB + (tci + 1) * 128]
                    iv = rep[:, sp * 2 * SLAB + tci * 128: sp * 2 * SLAB + (tci + 1) * 128]
                    mv_ = rep[:, sp * 2 * SLAB + SLAB + tci * 128: sp * 2 * SLAB + SLAB + (tci + 1) * 128]
                    V.tensor_tensor(dst, src, iv, ALU.mult)
                    last = V.tensor_tensor(dst, dst, mv_, ALU.subtract)
                return last
            vx = p.op("v", fxln, waits=wv, inc=True)
            if tci == 3:
                rep_used[sp] = vx
                xbf_used[sp] = vx
            ev = None
            for js in range(6):
                mp = (tc * 6 + js) % 2
                wmm = [("mv", vx)] if js == 0 else []
                if js == 0 and tc == 0:
                    wmm += [("dg", d_b), ("dg", d_w)]
                if pmm_free[mp]:
                    wmm.append(("ms", pmm_free[mp]))
                p.op("t", lambda T, js=js, mp=mp: T.matmul(
                    pmm[:, mp * SLAB:(mp + 1) * SLAB],
                    onesf[:], qkvb[0:1, js * SLAB:(js + 1) * SLAB],
                    start=True, stop=False), waits=wmm)
                mmc = None
                for kt in range(NPT):
                    mmc = p.op("t", lambda T, js=js, kt=kt, cp=cp, mp=mp: T.matmul(
                        pmm[:, mp * SLAB:(mp + 1) * SLAB],
                        xln1[:, cp * NPT * 128 + kt * 128: cp * NPT * 128 + (kt + 1) * 128],
                        qkvw[:, kt * 3 * C + js * SLAB: kt * 3 * C + (js + 1) * SLAB],
                        start=False, stop=(kt == NPT - 1)),
                        inc=(kt == NPT - 1))
                if js == 5:
                    xln_used[cp] = mmc
                wev = [("mt", mmc)]
                if js == 0 and cons_ms[cp]:
                    wev.append(("mv", cons_ms[cp]))
                if js >= 4 and kv_ms[cp]:
                    wev.append(("mt", kv_ms[cp]))
                if js < 4:
                    ev = p.op("s", lambda S, js=js, cp=cp, mp=mp: S.copy(
                        qkvbuf[:, cp * 2048 + js * SLAB: cp * 2048 + (js + 1) * SLAB],
                        pmm[:, mp * SLAB:(mp + 1) * SLAB]), waits=wev, inc=True)
                else:
                    ev = p.op("s", lambda S, js=js, cp=cp, mp=mp: S.copy(
                        vbf[:, cp * C + (js - 4) * SLAB: cp * C + (js - 3) * SLAB],
                        pmm[:, mp * SLAB:(mp + 1) * SLAB]), waits=wev, inc=True)
                pmm_free[mp] = ev
            wsqc = [("ms", ev)]
            if sqq_war:
                wsqc.append(("mv", sqq_war))
            sqc = p.op("s", lambda S, cp=cp: S.activation(
                sqq[:], qkvbuf[:, cp * 2048: cp * 2048 + 2048], AF.Square),
                waits=wsqc, inc=True)
            vred = p.op("v", lambda V: V.reduce_sum(
                l2s[:, 0:32],
                bass.AP(sqq, 0, [[2048, 128], [64, 32], [1, 64]]), AX.X),
                waits=[("ms", sqc)], inc=True)
            sqq_war = vred
            snrm = p.op("s", lambda S: S.activation(
                l2s[:, 32:64], l2s[:, 0:32], AF.Sqrt),
                waits=[("mv", vred)], inc=True)

            vr0 = p.op("v", lambda V: V.tensor_scalar_max(
                l2s[:, 32:64], l2s[:, 32:64], 1e-12),
                waits=[("ms", snrm)], inc=True)
            vrec = p.op("v", lambda V: V.reciprocal(l2s[:, 64:96], l2s[:, 32:64]),
                        waits=[("mv", vr0)], inc=True)
            wsc = [("mv", vrec)]
            if kv_ms[cp]:
                wsc.append(("mt", kv_ms[cp]))
            if qspill_war[cp]:
                wsc.append(("dq", qspill_war[cp]))

            def fscale(V, cp=cp):
                last = None
                for h in range(32):
                    last = V.tensor_single_scalar(
                        qkn[:, cp * 2048 + h * 64: cp * 2048 + (h + 1) * 64],
                        qkvbuf[:, cp * 2048 + h * 64: cp * 2048 + (h + 1) * 64],
                        l2s[:, 64 + h:65 + h], ALU.mult)
                return last
            vsc = p.op("v", fscale, waits=wsc, inc=True)
            cons_ms[cp] = vsc
            for hp in range(NPT):
                wkv = [("mv", vsc)] if hp == 0 else []
                if hp == 0 and tc == 16:
                    wkv.append(("ms", kv_ev))
                last_kv = p.op("t", lambda T, hp=hp, cp=cp, tc=tc: T.matmul(
                    pkv[:, hp * 128:(hp + 1) * 128],
                    qkn[:, cp * 2048 + 1024 + hp * 128: cp * 2048 + 1024 + (hp + 1) * 128],
                    vbf[:, cp * C + hp * 128: cp * C + (hp + 1) * 128],
                    start=(tc % 16 == 0 and hp % 4 == 0),
                    stop=(tc % 16 == 15)),
                    waits=wkv, inc=(hp == NPT - 1))
            kv_ms[cp] = last_kv
            if tc == 15 or tc == 31:
                bb = tc // 16
                for hp in range(NPT):
                    wv2 = [("mt", last_kv)] if hp == 0 else []
                    p.op("s", lambda S, hp=hp, bb=bb: S.copy(
                        kvw[0:64, bb * NPT * 128 + hp * 128: bb * NPT * 128 + hp * 128 + 64],
                        pkv[0:64, hp * 128: hp * 128 + 64]), waits=wv2)
                    kv_ev = p.op("s", lambda S, hp=hp, bb=bb: S.copy(
                        kvw[64:128, bb * NPT * 128 + hp * 128 + 64: bb * NPT * 128 + hp * 128 + 128],
                        pkv[64:128, hp * 128 + 64: hp * 128 + 128]), inc=True)
            q_spill_dma = p.dma("q", _ap2(q_dram, tc * 128, 128, 0, C, C),
                                qkn[:, cp * 2048: cp * 2048 + 1024],
                                waits=[("mv", vsc)])
            qspill_war[cp] = q_spill_dma

    if debug:
        p.dma("q", kvw_dbg, kvw[:], waits=[("ms", kv_ev)])

    qT_used = [0] * 2
    aoT_used = [0] * 2
    pout_free = [0] * 2
    pmm2_free = [0] * 2
    xold_war = [0] * 2
    xout_war = [(0, 0)] * 2
    x2bf_war = [0] * 2
    stats_t_ms = [0] * 2
    x2sq_war = [0] * 2
    rep2_used = [0] * 2
    ln2_vc_prev = 0
    rowdma2_prev = 0
    mx_ms = 0
    for s in range(NSLAB):
        sp = s % 2
        wq = [("dq", q_spill_dma)] if s == 0 else []
        if qT_used[0]:
            wq.append(("mt", qT_used[0]))
        d_qt = None
        for db in range(NPT):
            d_qt = p.dmat("s", qT[:, db * SLAB:(db + 1) * SLAB],
                          _ap2(q_dram, s * SLAB, SLAB, db * 128, 128, C),
                          waits=(wq if db == 0 else []))
        eo = None
        for hp in range(NPT):
            op_ = hp % 2
            w = [("ds", d_qt)] if hp == 0 else []
            if hp == 0:
                w.append(("ms", kv_ev))
            if pout_free[op_]:
                w.append(("ms", pout_free[op_]))
            bb = s // 4
            mo = p.op("t", lambda T, hp=hp, sp=sp, op_=op_, bb=bb: T.matmul(
                pkv[:, op_ * SLAB:(op_ + 1) * SLAB],
                kvw[:, bb * NPT * 128 + hp * 128: bb * NPT * 128 + (hp + 1) * 128],
                qT[:, hp * SLAB:(hp + 1) * SLAB],
                start=True, stop=True), waits=w, inc=True)
            wev = [("mt", mo)]
            if hp < 2 and aoT_used[sp]:
                wev.append(("mt", aoT_used[sp]))
            eo = p.op("s", lambda S, hp=hp, sp=sp, op_=op_: S.copy(
                aoT[:, sp * NPT * SLAB + hp * SLAB: sp * NPT * SLAB + (hp + 1) * SLAB],
                pkv[:, op_ * SLAB:(op_ + 1) * SLAB]), waits=wev, inc=True)
            pout_free[op_] = eo
            if debug:
                p.dma("q", _ap2(aot_dbg, hp * 128, 128, s * SLAB, SLAB, NTOK),
                      aoT[:, sp * NPT * SLAB + hp * SLAB: sp * NPT * SLAB + (hp + 1) * SLAB],
                      waits=[("ms", eo)])
            if hp == NPT - 1:
                qT_used[0] = mo
        mm2 = None
        for jc in range(NPT):
            mp = jc % 2
            wxo = []
            if xold_war[mp]:
                wxo.append(("mv", xold_war[mp]))
            d_xo = p.dma("q", xold[:, mp * SLAB:(mp + 1) * SLAB],
                         _ap2(x_in, jc * 128, 128, s * SLAB, SLAB, NTOK),
                         waits=wxo)
            mpr = None
            for kt in range(NPT):
                w = []
                if jc == 0 and kt == 0:
                    w.append(("ms", eo))
                    if s == 0:
                        w.append(("dg", d_pw))
                if kt == 0 and pmm2_free[mp]:
                    w.append(("mv", pmm2_free[mp]))
                mpr = p.op("t", lambda T, jc=jc, kt=kt, sp=sp, mp=mp: T.matmul(
                    pmm[:, mp * SLAB:(mp + 1) * SLAB],
                    projw[:, kt * C + jc * 128: kt * C + (jc + 1) * 128],
                    aoT[:, sp * NPT * SLAB + kt * SLAB: sp * NPT * SLAB + (kt + 1) * SLAB],
                    start=(kt == 0), stop=(kt == NPT - 1)),
                    waits=w, inc=(kt == NPT - 1))
            if jc == NPT - 1:
                aoT_used[sp] = mpr
            wst = [("mt", mpr), ("dq", d_xo)]
            if s == 0 and jc == 0:
                wst.append(("dg", d_pb))
            dqw, msw = xout_war[mp]
            if dqw:
                wst.append(("dq", dqw))
            if msw:
                wst.append(("ms", msw))
            vst = p.op("v", lambda V, jc=jc, mp=mp: V.scalar_tensor_tensor(
                xout[:, mp * SLAB:(mp + 1) * SLAB],
                pmm[:, mp * SLAB:(mp + 1) * SLAB],
                projb[:, jc:jc + 1],
                xold[:, mp * SLAB:(mp + 1) * SLAB],
                ALU.add, ALU.add), waits=wst, inc=True)
            pmm2_free[mp] = vst
            xold_war[mp] = vst
            dq_st = p.dma("q", _ap2(x2_out, jc * 128, 128, s * SLAB, SLAB, NTOK),
                          xout[:, mp * SLAB:(mp + 1) * SLAB], waits=[("mv", vst)])
            wbf = [("mv", vst)]
            if jc < 2 and x2bf_war[sp]:
                wbf.append(("mv", x2bf_war[sp]))
            if jc < 2 and stats_t_ms[sp]:
                wbf.append(("mt", stats_t_ms[sp]))
            sbf = p.op("s", lambda S, jc=jc, mp=mp, sp=sp: S.copy(
                x2bf[:, sp * NPT * SLAB + jc * SLAB: sp * NPT * SLAB + (jc + 1) * SLAB],
                xout[:, mp * SLAB:(mp + 1) * SLAB]), waits=wbf, inc=True)
            xout_war[mp] = (dq_st, sbf)
            wsq = [("ms", sbf)]
            if x2sq_war[mp]:
                wsq.append(("mt", x2sq_war[mp]))
            s2 = p.op("s", lambda S, sp=sp, jc=jc, mp=mp: S.activation(
                x2sq[:, mp * SLAB:(mp + 1) * SLAB],
                x2bf[:, sp * NPT * SLAB + jc * SLAB: sp * NPT * SLAB + (jc + 1) * SLAB],
                AF.Square), waits=wsq, inc=True)
            wmm = [("ms", s2)]
            if s == 0 and jc == 0:
                wmm.append(("mv", prev_vc))
            if jc == 0 and ln2_vc_prev:
                wmm.append(("mv", ln2_vc_prev))
            p.op("t", lambda T, sp=sp, jc=jc: T.matmul(
                pst[0:1, 0:SLAB], ones_bf[:],
                x2bf[:, sp * NPT * SLAB + jc * SLAB: sp * NPT * SLAB + (jc + 1) * SLAB],
                start=(jc == 0), stop=(jc == NPT - 1)), waits=wmm)
            mm2 = p.op("t", lambda T, jc=jc, mp=mp: T.matmul(
                pst[0:1, SLAB:2 * SLAB], ones_bf[:],
                x2sq[:, mp * SLAB:(mp + 1) * SLAB],
                start=(jc == 0), stop=(jc == NPT - 1)), inc=True)
            x2sq_war[mp] = mm2
        stats_t_ms[sp] = mm2
        co = 0
        wrm = [("dq", rowdma2_prev)] if rowdma2_prev else [("dg", rowdma_prev)]
        ln2_vc_prev, vc2 = emit_ln_rowmath(p, rows, co, pst, mm2, epscol, extra_waits=wrm)
        p.dma("q", _ap2(stat2_out, 0, 1, s * SLAB, SLAB, NTOK),
              rows[0:1, co + 2 * SLAB:co + 3 * SLAB], waits=[("mv", vc2)])
        p.dma("q", _ap2(stat2_out, 1, 1, s * SLAB, SLAB, NTOK),
              rows[0:1, co + 3 * SLAB:co + 4 * SLAB])
        p.dma("q", _ap2(stat2_dram, 0, 1, s * SLAB, SLAB, NTOK),
              rows[0:1, co + 2 * SLAB:co + 3 * SLAB])
        dso = p.dma("q", _ap2(stat2_dram, 1, 1, s * SLAB, SLAB, NTOK),
                    rows[0:1, co + 3 * SLAB:co + 4 * SLAB])
        rowdma2_prev = dso
        wrep = [("dq", dso)]
        if rep2_used[sp]:
            wrep.append(("mv", rep2_used[sp]))
        if s == 0:
            wrep.append(("mv", rep_used[0]))
            wrep.append(("mv", rep_used[1]))
        p.dma("g", rep[:, sp * 2 * SLAB: sp * 2 * SLAB + SLAB],
              _bcast_row(stat2_dram, 0 * NTOK + s * SLAB, SLAB), waits=wrep)
        d_r2 = p.dma("g", rep[:, sp * 2 * SLAB + SLAB:(sp + 1) * 2 * SLAB],
                     _bcast_row(stat2_dram, 1 * NTOK + s * SLAB, SLAB))
        for jc in range(NPT):
            tp = jc % 2

            def fmx(V, sp=sp, jc=jc, tp=tp):
                t1 = tmax[:, tp * SLAB:(tp + 1) * SLAB]
                xb = x2bf[:, sp * NPT * SLAB + jc * SLAB: sp * NPT * SLAB + (jc + 1) * SLAB]
                V.tensor_tensor(t1, xb, rep[:, sp * 2 * SLAB: sp * 2 * SLAB + SLAB], ALU.mult)
                V.tensor_tensor(t1, t1, rep[:, sp * 2 * SLAB + SLAB:(sp + 1) * 2 * SLAB], ALU.subtract)
                return V.reduce_max(mxt[:], t1, AX.X, apply_absolute_value=True)
            wmx = [("dg", d_r2)] if jc == 0 else []
            red_ms = p.op("v", fmx, waits=wmx, inc=True)
            mx_ms = p.op("v", lambda V: V.tensor_max(mxacc[:], mxacc[:], mxt[:]),
                         waits=[("mv", red_ms)], inc=True)
        x2bf_war[sp] = mx_ms
        rep2_used[sp] = mx_ms
        if debug and s == NSLAB - 1:
            p.dma("q", rep_dbg, rep[:, sp * 2 * SLAB:(sp + 1) * 2 * SLAB],
                  waits=[("mv", mx_ms)])
            p.dma("q", tmax_dbg, tmax[:], waits=[("mv", mx_ms)])

    if debug:
        p.dma("q", mx_dbg, mxacc[:], waits=[("mv", mx_ms)])
    emit_locmax_finish(p, bufs, mxacc, mxd_dram, lm_out, [("mv", mx_ms)])

    p.wait("q", [("dq", p.counts["dq"])])
    p.wait("g", [("dg", p.counts["dg"])])
    p.wait("s", [("ds", p.counts["ds"])])

    p.check()
    with es:
        p.realize(nc, es)
    return nc



def build_L2(upto="full", sub=4, debug=False):
    nc = bass.Bass("TRN2", target_bir_lowering=False)
    x2_in = nc.dram_tensor("x2", [C, NTOK], F32, kind="ExternalInput").ap()
    stat_in = nc.dram_tensor("stat", [2, NTOK], F32, kind="ExternalInput").ap()
    maxv_in = nc.dram_tensor("maxv", [1, NCORES], F32, kind="ExternalInput").ap()
    clip_in = nc.dram_tensor("clip", [1, 1], F32, kind="ExternalInput").ap()
    wg_in = nc.dram_tensor("w_gate", [I_FFN, C], F32, kind="ExternalInput").ap()
    wu_in = nc.dram_tensor("w_up", [I_FFN, C], F32, kind="ExternalInput").ap()
    h_out = nc.dram_tensor("h", [I_FFN, NTOK], F32, kind="ExternalOutput").ap()
    fvec_out = nc.dram_tensor("fvec", [128, NIC], F32, kind="ExternalOutput").ap()
    lm_out = nc.dram_tensor("locmax", [1, 1], F32, kind="ExternalOutput").ap()
    kd = "ExternalOutput" if debug else "Internal"
    wgi_dram = nc.dram_tensor("wgi_dram", [I_FFN, C], BF16, kind=kd).ap()
    wui_dram = nc.dram_tensor("wui_dram", [I_FFN, C], BF16, kind=kd).ap()
    scr = nc.dram_tensor("scr", [1, 4], F32).ap()

    es = ExitStack()
    sb = lambda n, shp, dt=F32: es.enter_context(nc.sbuf_tensor(n, shp, dt))
    psf = lambda n, shp: es.enter_context(nc.psum_tensor(n, shp, F32))
    bufs = {}
    xint = sb("xint", [128, NPT * NTOK], BF16)
    wtile = sb("wtile", [128, 2 * C], F32)
    wq = sb("wq", [128, 2 * C], F32)
    wint = sb("wint", [128, 2 * C], BF16)
    wgT = sb("wgT", [128, 2 * 2 * NPT * 128], BF16)
    swg = sb("swg", [128, NIC], F32)
    swu = sb("swu", [128, NIC], F32)
    sgx = sb("sgx", [128, NIC], F32)
    fvec = sb("fvec_sb", [128, NIC], F32)
    wmx = sb("wmx", [128, 2], F32)
    s8 = sb("s8", [1, 8], F32)
    sclip = sb("sclip", [1, 2], F32)
    sc1 = sb("sc1", [1, 4], F32)
    iscol = sb("iscol", [128, 1], F32)
    sxcol = sb("sxcol", [128, 1], F32)
    mcol = sb("mcol", [128, 2], F32)
    bufs["identf"] = sb("identf", [128, 128], F32)
    bufs["gmx"] = sb("gmx", [1, 1], F32)
    mxacc = sb("mxacc", [128, 1], F32)
    mxt = sb("mxt", [128, 1], F32)
    hm = sb("hm", [128, 1], F32)
    rep = sb("rep", [128, 2 * 2 * SLAB], F32)
    x2t = sb("x2t", [128, 2 * SLAB], F32)
    xlnt = sb("xlnt", [128, 2 * SLAB], F32)
    y1t = sb("y1t", [128, 2 * SLAB], F32)
    gs = sb("gs", [128, 2 * SLAB], F32)
    hsb = sb("hsb", [128, 2 * SLAB], F32)
    pg = psf("pg", [128, 2 * SLAB])
    pu = psf("pu", [128, 2 * SLAB])
    mxd_dram = nc.dram_tensor("mxd_dram", [128, 1], F32).ap()
    bufs["mxrow"] = sb("mxrow", [1, 128], F32)

    p = Prog()

    def fsetup(g):
        g.memset(mcol[:, 0:1], MAGIC)
        g.memset(mcol[:, 1:2], -MAGIC)
        g.memset(mxacc[:], 0.0)
        g.memset(bufs["identf"][:], 0.0)
        return g.affine_select(out=bufs["identf"][:], in_=bufs["identf"][:],
                               compare_op=ALU.not_equal, fill=1.0, base=0,
                               pattern=[[-1, 128]], channel_multiplier=1)
    g0 = p.op("g", fsetup, inc=True)

    d_mx = p.dma("g", s8[:], maxv_in)
    d_cl = p.dma("g", sclip[:, 0:1], clip_in)
    vgm = p.op("v", lambda V: V.reduce_max(bufs["gmx"][:], s8[:], AX.X),
               waits=[("dg", d_mx)], inc=True)
    ssig = p.op("s", lambda S: S.activation(sclip[:, 1:2], sclip[:, 0:1],
                                            AF.Sigmoid), waits=[("dg", d_cl)], inc=True)

    vs1 = p.op("v", lambda V: V.tensor_tensor(
        sc1[0:1, 2:3], bufs["gmx"][:], sclip[0:1, 1:2], ALU.mult),
        waits=[("mv", vgm), ("ms", ssig)], inc=True)
    vs2 = p.op("v", lambda V: V.tensor_scalar_mul(
        sc1[0:1, 0:1], sc1[0:1, 2:3], 1.0 / QMAX), waits=[("mv", vs1)], inc=True)
    vsc = p.op("v", lambda V: V.reciprocal(sc1[0:1, 1:2], sc1[0:1, 0:1]),
               waits=[("mv", vs2)], inc=True)
    d_scr = p.dma("g", scr, sc1[:, 0:4], waits=[("mv", vsc)])
    d_isc = p.dma("g", iscol[:], _bcast_row(scr, 1, 1), waits=[("dg", d_scr)])
    d_sxc = p.dma("g", sxcol[:], _bcast_row(scr, 0, 1))

    wiq_dma = {"g": 0, "u": 0}
    wt_war = [0, 0]
    wq_war = [0, 0]
    wmx_war = [0, 0]
    wi_war = [0, 0]
    for idx, (tag, w_in, wo_dram, sdst) in enumerate(
            [("g", wg_in, wgi_dram, swg), ("u", wu_in, wui_dram, swu)]):
        for ic in range(NIC):
            par = (idx * NIC + ic) % 2
            w = []
            if wt_war[par]:
                w.append(("ms", wt_war[par]))
            d_wt = p.dma("g", wtile[:, par * C:(par + 1) * C],
                         _ap2(w_in, ic * 128, 128, 0, C, C), waits=w)

            wv = [("dg", d_wt), ("mg", g0)]
            if wmx_war[par]:
                wv.append(("ms", wmx_war[par]))
            vq0 = p.op("v", lambda V, par=par: V.reduce_max(
                wmx[:, par:par + 1], wtile[:, par * C:(par + 1) * C], AX.X,
                apply_absolute_value=True), waits=wv, inc=True)

            vqa = p.op("v", lambda V, par=par, ic=ic, sdst=sdst:
                       V.tensor_scalar_mul(sdst[:, ic:ic + 1],
                                           wmx[:, par:par + 1], 1.0 / QMAX),
                       waits=[("mv", vq0)], inc=True)
            vq = p.op("v", lambda V, par=par, ic=ic, sdst=sdst:
                      V.reciprocal(wmx[:, par:par + 1], sdst[:, ic:ic + 1]),
                      waits=[("mv", vqa)], inc=True)
            wsq_ = [("mv", vq)]
            if wq_war[par]:
                wsq_.append(("mv", wq_war[par]))
            sq = p.op("s", lambda S, par=par: S.activation(
                wq[:, par * C:(par + 1) * C], wtile[:, par * C:(par + 1) * C],
                AF.Identity, scale=wmx[:, par:par + 1], bias=mcol[:, 0:1]),
                waits=wsq_, inc=True)
            wt_war[par] = sq
            wmx_war[par] = sq
            wvi = [("ms", sq)]
            if wi_war[par]:
                wvi.append(("dg", wi_war[par]))
            vi = p.op("v", lambda V, par=par: V.tensor_single_scalar(
                wint[:, par * C:(par + 1) * C], wq[:, par * C:(par + 1) * C],
                MAGIC, ALU.subtract), waits=wvi, inc=True)
            wq_war[par] = vi
            dmaw = p.dma("g", _ap2(wo_dram, ic * 128, 128, 0, C, C),
                         wint[:, par * C:(par + 1) * C], waits=[("mv", vi)])
            wi_war[par] = dmaw
            wiq_dma[tag] = dmaw

    vsg = p.op("v", lambda V: V.tensor_single_scalar(
        sgx[:, 0:NIC], swg[:, 0:NIC], sxcol[:, 0:1], ALU.mult),
        waits=[("mv", p.counts["mv"]), ("dg", d_sxc)], inc=True)
    vfv = p.op("v", lambda V: V.tensor_single_scalar(
        fvec[:, 0:NIC], swu[:, 0:NIC], sxcol[:, 0:1], ALU.mult), inc=True)
    p.dma("q", fvec_out, fvec[:, 0:NIC], waits=[("mv", vfv)])

    if upto == "wq":
        emit_locmax_finish(p, bufs, mxacc, mxd_dram, lm_out, [("mg", g0)])
        p.wait("q", [("dq", p.counts["dq"])])
        p.wait("g", [("dg", p.counts["dg"])])
        p.wait("s", [("ds", p.counts["ds"])])
        with es:
            p.realize(nc, es)
        return nc

    x2_war = [0, 0]
    xln_war = [0, 0]
    y1_war = [0, 0]
    rep_war = [0, 0]
    xint_ready = None
    for s in range(NSLAB):
        sp = s % 2
        w = []
        if rep_war[sp]:
            w.append(("mv", rep_war[sp]))
        p.dma("g", rep[:, sp * 2 * SLAB: sp * 2 * SLAB + SLAB],
              _bcast_row(stat_in, 0 * NTOK + s * SLAB, SLAB), waits=w)
        d_rep = p.dma("g", rep[:, sp * 2 * SLAB + SLAB:(sp + 1) * 2 * SLAB],
                      _bcast_row(stat_in, 1 * NTOK + s * SLAB, SLAB))
        for pt in range(NPT):
            tp = (s * NPT + pt) % 2
            w = []
            if x2_war[tp]:
                w.append(("mv", x2_war[tp]))
            d_x2 = p.dma("q", x2t[:, tp * SLAB:(tp + 1) * SLAB],
                         _ap2(x2_in, pt * 128, 128, s * SLAB, SLAB, NTOK),
                         waits=w)

            def fxln(V, tp=tp, sp=sp):
                V.tensor_tensor(xlnt[:, tp * SLAB:(tp + 1) * SLAB],
                                x2t[:, tp * SLAB:(tp + 1) * SLAB],
                                rep[:, sp * 2 * SLAB: sp * 2 * SLAB + SLAB], ALU.mult)
                return V.tensor_tensor(xlnt[:, tp * SLAB:(tp + 1) * SLAB],
                                       xlnt[:, tp * SLAB:(tp + 1) * SLAB],
                                       rep[:, sp * 2 * SLAB + SLAB:(sp + 1) * 2 * SLAB],
                                       ALU.subtract)
            if sub < 1:
                continue
            wv = [("dq", d_x2), ("dg", d_rep)]
            if xln_war[tp]:
                wv.append(("ms", xln_war[tp]))
            vx = p.op("v", fxln, waits=wv, inc=True)
            x2_war[tp] = vx
            if pt == NPT - 1:
                rep_war[sp] = vx
            xint_ready = vx
            if sub < 2:
                continue
            wy = [("mv", vx), ("dg", d_isc)]
            if y1_war[tp]:
                wy.append(("mv", y1_war[tp]))
            sy1 = p.op("s", lambda S, tp=tp: S.activation(
                y1t[:, tp * SLAB:(tp + 1) * SLAB],
                xlnt[:, tp * SLAB:(tp + 1) * SLAB],
                AF.Identity, scale=iscol[:, 0:1], bias=mcol[:, 0:1]),
                waits=wy, inc=True)
            xln_war[tp] = sy1
            xint_ready = sy1
            if sub < 3:
                continue

            def fclip(V, tp=tp, pt=pt, s=s):
                V.tensor_scalar(xlnt[:, tp * SLAB:(tp + 1) * SLAB],
                                y1t[:, tp * SLAB:(tp + 1) * SLAB],
                                MAGIC, QMAX, ALU.subtract, ALU.min)
                return V.tensor_scalar_max(
                    xint[:, pt * NTOK + s * SLAB: pt * NTOK + (s + 1) * SLAB],
                    xlnt[:, tp * SLAB:(tp + 1) * SLAB], -QMAX - 1.0)
            vcl = p.op("v", fclip, waits=[("ms", sy1)], inc=True)
            y1_war[tp] = vcl
            xint_ready = vcl

    if upto == "xq":
        lw = [("mv", xint_ready)] if sub != 2 else [("ms", xint_ready)]
        emit_locmax_finish(p, bufs, mxacc, mxd_dram, lm_out, lw)
        p.wait("q", [("dq", p.counts["dq"])])
        p.wait("g", [("dg", p.counts["dg"])])
        p.wait("s", [("ds", p.counts["ds"])])
        with es:
            p.realize(nc, es)
        return nc

    wgT_war = [0, 0]
    pg_free = [0, 0]
    pu_free = [0, 0]
    gs_war = [0, 0]
    h_war = [0, 0]
    hdma = 0
    hm_done = []
    for ic in range(NIC):
        icp = ic % 2
        w = [("dg", max(wiq_dma["g"], wiq_dma["u"]))] if ic == 0 else []
        if wgT_war[icp]:
            w.append(("mt", wgT_war[icp]))
        d_wt = None
        for kt in range(NPT):
            d_wt = p.dmat("s", wgT[:, icp * 2 * NPT * 128 + kt * 128: icp * 2 * NPT * 128 + (kt + 1) * 128],
                          _ap2(wgi_dram, ic * 128, 128, kt * 128, 128, C),
                          waits=(w if kt == 0 else []))
        for kt in range(NPT):
            d_wt = p.dmat("s", wgT[:, icp * 2 * NPT * 128 + NPT * 128 + kt * 128: icp * 2 * NPT * 128 + NPT * 128 + (kt + 1) * 128],
                          _ap2(wui_dram, ic * 128, 128, kt * 128, 128, C))
        first_mm = None
        hmx_ms = 0
        for s in range(NSLAB):
            mp = s % 2
            mg_cnt = mu_cnt = None
            for kt in range(NPT):
                wmm = []
                if kt == 0 and s == 0:
                    wmm = [("ds", d_wt), ("mv", xint_ready)]
                if kt == 0 and pg_free[mp]:
                    wmm.append(("ms", pg_free[mp]))
                mg_cnt = p.op("t", lambda T, ic=ic, icp=icp, kt=kt, s=s, mp=mp: T.matmul(
                    pg[:, mp * SLAB:(mp + 1) * SLAB],
                    wgT[:, icp * 2 * NPT * 128 + kt * 128: icp * 2 * NPT * 128 + (kt + 1) * 128],
                    xint[:, kt * NTOK + s * SLAB: kt * NTOK + (s + 1) * SLAB],
                    start=(kt == 0), stop=(kt == NPT - 1)),
                    waits=wmm, inc=(kt == NPT - 1))
                if first_mm is None:
                    first_mm = mg_cnt
                wmm2 = []
                if kt == 0 and pu_free[mp]:
                    wmm2.append(("mv", pu_free[mp]))
                mu_cnt = p.op("t", lambda T, ic=ic, icp=icp, kt=kt, s=s, mp=mp: T.matmul(
                    pu[:, mp * SLAB:(mp + 1) * SLAB],
                    wgT[:, icp * 2 * NPT * 128 + NPT * 128 + kt * 128: icp * 2 * NPT * 128 + NPT * 128 + (kt + 1) * 128],
                    xint[:, kt * NTOK + s * SLAB: kt * NTOK + (s + 1) * SLAB],
                    start=(kt == 0), stop=(kt == NPT - 1)),
                    waits=wmm2, inc=(kt == NPT - 1))
            if s == NSLAB - 1:
                wgT_war[icp] = mu_cnt
            wgs = [("mt", mg_cnt), ("mv", vsg)]
            if gs_war[mp]:
                wgs.append(("mv", gs_war[mp]))
            sgs = p.op("s", lambda S, ic=ic, mp=mp: S.activation(
                gs[:, mp * SLAB:(mp + 1) * SLAB],
                pg[:, mp * SLAB:(mp + 1) * SLAB],
                AF.Silu, scale=sgx[:, ic:ic + 1]), waits=wgs, inc=True)
            pg_free[mp] = sgs
            wh = [("ms", sgs), ("mt", mu_cnt)]
            if h_war[mp]:
                wh.append(("dq", h_war[mp]))

            def fh(V, mp=mp):
                return V.tensor_tensor(hsb[:, mp * SLAB:(mp + 1) * SLAB],
                                       gs[:, mp * SLAB:(mp + 1) * SLAB],
                                       pu[:, mp * SLAB:(mp + 1) * SLAB], ALU.mult)
            vh = p.op("v", fh, waits=wh, inc=True)
            gs_war[mp] = vh
            pu_free[mp] = vh

            hred = p.op("v", lambda V, mp=mp: V.reduce_max(
                mxt[:], hsb[:, mp * SLAB:(mp + 1) * SLAB], AX.X,
                apply_absolute_value=True), inc=True)
            if s == 0:
                hmx_ms = p.op("v", lambda V: V.tensor_copy(hm[:], mxt[:]),
                              waits=[("mv", hred)], inc=True)
            else:
                hmx_ms = p.op("v", lambda V: V.tensor_max(hm[:], hm[:], mxt[:]),
                              waits=[("mv", hred)], inc=True)
            hdma = p.dma("q", _ap2(h_out, ic * 128, 128, s * SLAB, SLAB, NTOK),
                         hsb[:, mp * SLAB:(mp + 1) * SLAB], waits=[("mv", vh)])
            h_war[mp] = hdma

        ff1 = p.op("v", lambda V, ic=ic: V.tensor_tensor(
            mxt[:], hm[:], fvec[:, ic:ic + 1], ALU.mult),
            waits=[("mv", vfv), ("mg", g0), ("mv", hmx_ms)], inc=True)
        hm_done.append(p.op("v", lambda V: V.tensor_max(
            mxacc[:], mxacc[:], mxt[:]), waits=[("mv", ff1)], inc=True))

    emit_locmax_finish(p, bufs, mxacc, mxd_dram, lm_out, [("mv", hm_done[-1])])

    p.wait("q", [("dq", p.counts["dq"])])
    p.wait("g", [("dg", p.counts["dg"])])
    p.wait("s", [("ds", p.counts["ds"])])

    p.check()
    with es:
        p.realize(nc, es)
    return nc



def build_L3(with_attnb=True, debug=False):
    nc = bass.Bass("TRN2", target_bir_lowering=False)
    h_in = nc.dram_tensor("h", [I_FFN, NTOK], F32, kind="ExternalInput").ap()
    fvec_in = nc.dram_tensor("fvec", [128, NIC], F32, kind="ExternalInput").ap()
    maxv_in = nc.dram_tensor("maxv", [1, NCORES], F32, kind="ExternalInput").ap()
    clip_in = nc.dram_tensor("clip", [1, 1], F32, kind="ExternalInput").ap()
    x2_in = nc.dram_tensor("x2", [C, NTOK], F32, kind="ExternalInput").ap()
    wd_in = nc.dram_tensor("w_down", [C, I_FFN], F32, kind="ExternalInput").ap()
    if with_attnb:
        wq_in = nc.dram_tensor("Wq", [C, C], F32, kind="ExternalInput").ap()
        wk_in = nc.dram_tensor("Wk", [C, C], F32, kind="ExternalInput").ap()
        wv_in = nc.dram_tensor("Wv", [C, C], F32, kind="ExternalInput").ap()
        wo_in = nc.dram_tensor("Wo", [C, C], F32, kind="ExternalInput").ap()
        cwq_in = nc.dram_tensor("conv_wq", [C, 4], F32, kind="ExternalInput").ap()
        cwk_in = nc.dram_tensor("conv_wk", [C, 4], F32, kind="ExternalInput").ap()
        cwv_in = nc.dram_tensor("conv_wv", [C, 4], F32, kind="ExternalInput").ap()
        onw_in = nc.dram_tensor("onorm_w", [1, 256], F32, kind="ExternalInput").ap()
        mtri_in = nc.dram_tensor("mtri", [128, 384], F32, kind="ExternalInput").ap()
        x3b_out = nc.dram_tensor("x3b", [C, NTOK], F32, kind="ExternalOutput").ap()
        stat4_out = nc.dram_tensor("stat4", [2, NTOK], F32, kind="ExternalOutput").ap()
        lm_out = nc.dram_tensor("locmax", [1, 1], F32, kind="ExternalOutput").ap()
        x3_dram = nc.dram_tensor("x3_dram", [C, NTOK], F32,
                                 kind="ExternalOutput" if debug else "Internal").ap()
        if debug:
            qn_dbg = nc.dram_tensor("qn_dbg", [128, NPT * SLAB], BF16, kind="ExternalOutput").ap()
            sc_dbg = nc.dram_tensor("sc_dbg", [128, 2048], BF16, kind="ExternalOutput").ap()
            x3dbg = nc.dram_tensor("x3dbg", [128, NPT * SLAB], BF16, kind="ExternalOutput").ap()
            zdbg = nc.dram_tensor("zdbg", [128, 3 * (SLAB + 4)], F32, kind="ExternalOutput").ap()
            cvdbg = nc.dram_tensor("cvdbg", [128, SLAB], F32, kind="ExternalOutput").ap()
            rdbg = nc.dram_tensor("rdbg", [1, 2 * SLAB], F32, kind="ExternalOutput").ap()
            sc2_dbg = nc.dram_tensor("sc2_dbg", [128, 2048], BF16, kind="ExternalOutput").ap()
            bev_dbg = nc.dram_tensor("bev_dbg", [128, 7 * 256], BF16, kind="ExternalOutput").ap()
            msk_dbg = nc.dram_tensor("msk_dbg", [128, 384], F32, kind="ExternalOutput").ap()
            sf_dbg = nc.dram_tensor("sf_dbg", [128, 2048], F32, kind="ExternalOutput").ap()
            st3_dbg = nc.dram_tensor("st3_dbg", [8, 1024], F32, kind="ExternalOutput").ap()
            kn_dbg = nc.dram_tensor("kn_dbg", [128, NPT * SLAB], BF16, kind="ExternalOutput").ap()
            vn_dbg = nc.dram_tensor("vn_dbg", [128, NPT * SLAB], BF16, kind="ExternalOutput").ap()
            onT_dbg = nc.dram_tensor("onT_dbg", [128, NPT * SLAB], BF16, kind="ExternalOutput").ap()
        stat3_dram = nc.dram_tensor("stat3_dram", [2, NTOK], F32).ap()
        stat4_dram = nc.dram_tensor("stat4_dram", [2, NTOK], F32).ap()
        mxd_dram = nc.dram_tensor("mxd_dram", [128, 1], F32).ap()
    else:
        y_out = nc.dram_tensor("y", [C, NTOK], F32, kind="ExternalOutput").ap()
    wd_int_dram = nc.dram_tensor("wd_int_dram", [C, I_FFN], BF16).ap()
    scr = nc.dram_tensor("scr", [1, 4], F32).ap()

    es = ExitStack()
    sb = lambda n, shp, dt=F32: es.enter_context(nc.sbuf_tensor(n, shp, dt))
    psf = lambda n, shp: es.enter_context(nc.psum_tensor(n, shp, F32))
    bufs = {}
    wtile = sb("wtile", [128, 2 * I_FFN], F32)
    wint = sb("wint", [128, 2 * I_FFN], BF16)
    wdT = sb("wdT", [128, 2 * NIC * 128], BF16)
    swd = sb("swd", [128, NPT], F32)
    sdw = sb("sdw", [128, NPT], F32)
    qfq = sb("qfq", [128, NIC], F32)
    fvec = sb("fvec_sb", [128, NIC], F32)
    wmx = sb("wmx", [128, 2], F32)
    s8 = sb("s8", [1, 8], F32)
    sclip = sb("sclip", [1, 2], F32)
    sc1 = sb("sc1", [1, 4], F32)
    iscol = sb("iscol", [128, 1], F32)
    sdcol = sb("sdcol", [128, 1], F32)
    mcol = sb("mcol", [128, 2], F32)
    epscol = sb("epscol", [128, 2], F32)
    bufs["gmx"] = sb("gmx", [1, 1], F32)
    bufs["mxrow"] = sb("mxrow", [1, 128], F32)
    ones_bf = sb("ones_bf", [128, 1], BF16)
    onesf = sb("onesf", [1, 128], BF16)
    hld = sb("hld", [128, 2 * SLAB], F32)
    y1t = sb("y1t", [128, 2 * SLAB], F32)
    hint = sb("hint", [128, NIC * SLAB], BF16)
    x2t = sb("x2t", [128, 2 * SLAB], F32)
    x3t = sb("x3t", [128, 2 * SLAB], F32)
    x3bt = sb("x3bt", [128, 2 * SLAB], BF16)
    rows = sb("rows", [1, 4 * SLAB], F32)
    pd = psf("pd", [128, 2 * SLAB])
    pst = psf("pst", [128, 1024])
    p = Prog()
    if with_attnb:
        rep = sb("rep", [128, 2 * 2 * SLAB], F32)
        wstr = sb("wstr", [128, 2 * NPT * 128], BF16)
        xln3 = sb("xln3", [128, NPT * SLAB], BF16)
        zraw = sb("zraw", [128, 3 * (SLAB + 4)], F32)
        halo = sb("halo", [128, 3 * NPT * 4], F32)
        cwall = sb("cwall", [128, 3 * 4 * NPT], F32)
        qn = sb("qn", [128, NPT * SLAB], BF16)
        kn = sb("kn", [128, NPT * SLAB], BF16)
        vn = sb("vn", [128, NPT * SLAB], BF16)
        silt = sb("silt", [128, SLAB], F32)
        sqt = sb("sqt", [128, SLAB], BF16)
        lrow = sb("lrow", [1, 2 * SLAB], F32)
        lrbf = sb("lrbf", [1, SLAB], BF16)
        ktok = sb("ktok", [128, 2 * 256], BF16)
        vtok = sb("vtok", [128, 2 * 256], BF16)
        Sf = sb("Sf", [128, 4 * 512], F32)
        Sb = sb("Sb", [128, 4 * 512], BF16)
        Bc = sb("Bc", [128, 2 * 2 * 128], BF16)
        Xb = sb("Xb", [128, 2 * 256], BF16)
        RHSb = sb("RHSb", [128, 256], BF16)
        Ptb = sb("Ptb", [128, 128], BF16)
        o2b = sb("o2b", [128, 256], BF16)
        rrb = sb("rrb", [1, 128], BF16)
        onw = sb("onw", [128, 2], F32)
        onT = sb("onT", [128, NPT * SLAB], BF16)
        mlow = sb("mlow", [128, 128], F32)
        mupn = sb("mupn", [128, 128], F32)
        mupi = sb("mupi", [128, 128], F32)
        mxacc = sb("mxacc", [128, 1], F32)
        mxt = sb("mxt", [128, 1], F32)
        tmax = sb("tmax", [128, 2 * SLAB], F32)
        cvt = sb("cvt", [128, SLAB], F32)
        x4t = sb("x4t", [128, 2 * SLAB], F32)
        x4bf = sb("x4bf", [128, 2 * NPT * SLAB], BF16)
        x4sq = sb("x4sq", [128, 2 * SLAB], BF16)
        pscan = psf("pscan", [128, 512])
        pB = psf("pB", [128, 512])
        pC = psf("pC", [128, 512])

    def fsetup(g):
        g.memset(mcol[:, 0:1], MAGIC)
        g.memset(mcol[:, 1:2], -MAGIC)
        g.memset(epscol[:, 0:1], 1e-5)
        g.memset(epscol[:, 1:2], 1e-6)
        g.memset(ones_bf[:], 1.0)
        g.memset(onesf[:], 1.0)
        if with_attnb:
            g.memset(mxacc[:], 0.0)
            g.memset(halo[:], 0.0)
        return g.nop()
    g0 = p.op("g", fsetup, inc=True)
    if with_attnb:
        p.dma("g", mlow[:], bass.AP(mtri_in.tensor, 0, [[384, 128], [1, 128]]))
        p.dma("g", mupn[:], bass.AP(mtri_in.tensor, 128, [[384, 128], [1, 128]]))
        msk_dg = p.dma("g", mupi[:], bass.AP(mtri_in.tensor, 256, [[384, 128], [1, 128]]))

    d_mx = p.dma("g", s8[:], maxv_in)
    d_cl = p.dma("g", sclip[:, 0:1], clip_in)
    d_fv = p.dma("g", fvec[:], fvec_in)
    vgm = p.op("v", lambda V: V.reduce_max(bufs["gmx"][:], s8[:], AX.X),
               waits=[("dg", d_mx)], inc=True)
    ssig = p.op("s", lambda S: S.activation(sclip[:, 1:2], sclip[:, 0:1],
                                            AF.Sigmoid), waits=[("dg", d_cl)], inc=True)
    vs1 = p.op("v", lambda V: V.tensor_tensor(
        sc1[0:1, 2:3], bufs["gmx"][:], sclip[0:1, 1:2], ALU.mult),
        waits=[("mv", vgm), ("ms", ssig)], inc=True)
    vs2 = p.op("v", lambda V: V.tensor_scalar_mul(
        sc1[0:1, 0:1], sc1[0:1, 2:3], 1.0 / QMAX), waits=[("mv", vs1)], inc=True)
    vsc = p.op("v", lambda V: V.reciprocal(sc1[0:1, 1:2], sc1[0:1, 0:1]),
               waits=[("mv", vs2)], inc=True)
    d_scr = p.dma("g", scr, sc1[:, 0:4], waits=[("mv", vsc)])
    d_isc = p.dma("g", iscol[:], _bcast_row(scr, 1, 1), waits=[("dg", d_scr)])
    d_sdc = p.dma("g", sdcol[:], _bcast_row(scr, 0, 1))
    vqf = p.op("v", lambda V: V.tensor_single_scalar(
        qfq[:, 0:NIC], fvec[:, 0:NIC], iscol[:, 0:1], ALU.mult),
        waits=[("dg", max(d_isc, d_fv))], inc=True)

    wt_war = [0, 0]
    wq_war = [0, 0]
    wmx_war = [0, 0]
    wi_war = [0, 0]
    wdq_dma = 0
    for hc in range(NPT):
        par = hc % 2
        w = []
        if wt_war[par]:
            w.append(("ms", wt_war[par]))
        d_wt = p.dma("g", wtile[:, par * I_FFN:(par + 1) * I_FFN],
                     _ap2(wd_in, hc * 128, 128, 0, I_FFN, I_FFN), waits=w)
        wv = [("dg", d_wt), ("mg", g0)]
        if wmx_war[par]:
            wv.append(("ms", wmx_war[par]))
        vq0 = p.op("v", lambda V, par=par: V.reduce_max(
            wmx[:, par:par + 1], wtile[:, par * I_FFN:(par + 1) * I_FFN], AX.X,
            apply_absolute_value=True), waits=wv, inc=True)
        vqa = p.op("v", lambda V, par=par, hc=hc: V.tensor_scalar_mul(
            swd[:, hc:hc + 1], wmx[:, par:par + 1], 1.0 / QMAX),
            waits=[("mv", vq0)], inc=True)
        vq = p.op("v", lambda V, par=par, hc=hc: V.reciprocal(
            wmx[:, par:par + 1], swd[:, hc:hc + 1]), waits=[("mv", vqa)], inc=True)
        wsq_ = [("mv", vq)]
        if wq_war[par]:
            wsq_.append(("mv", wq_war[par]))
        sq = p.op("s", lambda S, par=par: S.activation(
            wtile[:, par * I_FFN:(par + 1) * I_FFN],
            wtile[:, par * I_FFN:(par + 1) * I_FFN],
            AF.Identity, scale=wmx[:, par:par + 1], bias=mcol[:, 0:1]),
            waits=wsq_, inc=True)
        wt_war[par] = sq
        wmx_war[par] = sq
        wvi = [("ms", sq)]
        if wi_war[par]:
            wvi.append(("dg", wi_war[par]))
        vi = p.op("v", lambda V, par=par: V.tensor_single_scalar(
            wint[:, par * I_FFN:(par + 1) * I_FFN],
            wtile[:, par * I_FFN:(par + 1) * I_FFN],
            MAGIC, ALU.subtract), waits=wvi, inc=True)
        wq_war[par] = vi
        dmaw = p.dma("g", _ap2(wd_int_dram, hc * 128, 128, 0, I_FFN, I_FFN),
                     wint[:, par * I_FFN:(par + 1) * I_FFN], waits=[("mv", vi)])
        wi_war[par] = dmaw
        wdq_dma = dmaw
    vsdw = p.op("v", lambda V: V.tensor_single_scalar(
        sdw[:, 0:NPT], swd[:, 0:NPT], sdcol[:, 0:1], ALU.mult),
        waits=[("mv", p.counts["mv"]), ("dg", d_sdc)], inc=True)

    h_war = [0, 0]
    y1_war = [0, 0]
    x2_war = [0, 0]
    x3_war = [(0, 0)] * 2
    wdT_war = [0, 0]
    pd_free = [0, 0]
    hint_war = 0
    stats3_done = []
    x3bt_war = [0, 0]
    x3sq_war = [0, 0]
    ln3_vc_prev = 0
    rowdma3 = []
    for s in range(NSLAB):
        vcl = None
        for ic in range(NIC):
            tp = (s * NIC + ic) % 2
            w = []
            if h_war[tp]:
                w.append(("ms", h_war[tp]))
            d_h = p.dma("q", hld[:, tp * SLAB:(tp + 1) * SLAB],
                        _ap2(h_in, ic * 128, 128, s * SLAB, SLAB, NTOK), waits=w)
            wy = [("dq", d_h), ("mv", vqf)]
            if y1_war[tp]:
                wy.append(("mv", y1_war[tp]))
            sy1 = p.op("s", lambda S, tp=tp, ic=ic: S.activation(
                y1t[:, tp * SLAB:(tp + 1) * SLAB],
                hld[:, tp * SLAB:(tp + 1) * SLAB],
                AF.Identity, scale=qfq[:, ic:ic + 1], bias=mcol[:, 0:1]),
                waits=wy, inc=True)
            h_war[tp] = sy1

            def fclip(V, tp=tp, ic=ic):
                V.tensor_scalar(y1t[:, tp * SLAB:(tp + 1) * SLAB],
                                y1t[:, tp * SLAB:(tp + 1) * SLAB],
                                MAGIC, QMAX, ALU.subtract, ALU.min)
                return V.tensor_scalar_max(
                    hint[:, ic * SLAB:(ic + 1) * SLAB],
                    y1t[:, tp * SLAB:(tp + 1) * SLAB], -QMAX - 1.0)
            wcl = [("ms", sy1)]
            if ic == 0 and hint_war:
                wcl.append(("mt", hint_war))
            vcl = p.op("v", fclip, waits=wcl, inc=True)
            y1_war[tp] = vcl
        mm2 = None
        sbf = None
        for hc in range(NPT):
            wp = hc % 2
            mp = hc % 2
            wdt = [("dg", wdq_dma)] if (s == 0 and hc == 0) else []
            if wdT_war[wp]:
                wdt.append(("mt", wdT_war[wp]))
            d_wdt = None
            for ikt in range(NIC):
                d_wdt = p.dmat("s", wdT[:, wp * NIC * 128 + ikt * 128: wp * NIC * 128 + (ikt + 1) * 128],
                               _ap2(wd_int_dram, hc * 128, 128, ikt * 128, 128, I_FFN),
                               waits=(wdt if ikt == 0 else []))
            mmd = None
            for ikt in range(NIC):
                wmm = []
                if ikt == 0:
                    wmm = [("ds", d_wdt), ("mv", vcl)]
                    if pd_free[mp]:
                        wmm.append(pd_free[mp])
                mmd = p.op("t", lambda T, ikt=ikt, mp=mp, wp=wp: T.matmul(
                    pd[:, mp * SLAB:(mp + 1) * SLAB],
                    wdT[:, wp * NIC * 128 + ikt * 128: wp * NIC * 128 + (ikt + 1) * 128],
                    hint[:, ikt * SLAB:(ikt + 1) * SLAB],
                    start=(ikt == 0), stop=(ikt == NIC - 1)),
                    waits=wmm, inc=(ikt == NIC - 1))
            wdT_war[wp] = mmd
            if hc == NPT - 1:
                hint_war = mmd
            wxo = []
            if x2_war[mp]:
                wxo.append(("mv", x2_war[mp]))
            d_x2 = p.dma("q", x2t[:, mp * SLAB:(mp + 1) * SLAB],
                         _ap2(x2_in, hc * 128, 128, s * SLAB, SLAB, NTOK),
                         waits=wxo)
            wst = [("mt", mmd), ("dq", d_x2), ("mv", vsdw)]
            dqw, msw = x3_war[mp]
            if dqw:
                wst.append((dqw[0], dqw[1]))
            if msw:
                wst.append(("ms", msw))
            vst = p.op("v", lambda V, hc=hc, mp=mp: V.scalar_tensor_tensor(
                x3t[:, mp * SLAB:(mp + 1) * SLAB],
                pd[:, mp * SLAB:(mp + 1) * SLAB],
                sdw[:, hc:hc + 1],
                x2t[:, mp * SLAB:(mp + 1) * SLAB],
                ALU.mult, ALU.add), waits=wst, inc=True)
            pd_free[mp] = ("mv", vst)
            x2_war[mp] = vst
            if not with_attnb:
                dq_st = p.dma("q", _ap2(y_out, hc * 128, 128, s * SLAB, SLAB, NTOK),
                              x3t[:, mp * SLAB:(mp + 1) * SLAB], waits=[("mv", vst)])
                x3_war[mp] = (("dq", dq_st), 0)
                continue
            dg_st = p.dma("g", _ap2(x3_dram, hc * 128, 128, s * SLAB, SLAB, NTOK),
                          x3t[:, mp * SLAB:(mp + 1) * SLAB], waits=[("mv", vst)])
            wbf = [("mv", vst)]
            if x3bt_war[mp]:
                wbf.append(("mt", x3bt_war[mp]))
            sbf = p.op("s", lambda S, mp=mp: S.copy(
                x3bt[:, mp * SLAB:(mp + 1) * SLAB],
                x3t[:, mp * SLAB:(mp + 1) * SLAB]), waits=wbf, inc=True)
            x3_war[mp] = (("dg", dg_st), sbf)
            wsq = [("ms", sbf)]
            if x3sq_war[mp]:
                wsq.append(("mt", x3sq_war[mp]))
            s2 = p.op("s", lambda S, mp=mp: S.activation(
                x4sq[:, mp * SLAB:(mp + 1) * SLAB],
                x3bt[:, mp * SLAB:(mp + 1) * SLAB],
                AF.Square), waits=wsq, inc=True)
            wmm = [("ms", s2)]
            if hc == 0 and ln3_vc_prev:
                wmm.append(("mv", ln3_vc_prev))
            p.op("t", lambda T, mp=mp, hc=hc: T.matmul(
                pst[0:1, 0:SLAB], ones_bf[:],
                x3bt[:, mp * SLAB:(mp + 1) * SLAB],
                start=(hc == 0), stop=(hc == NPT - 1)), waits=wmm)
            mm2 = p.op("t", lambda T, mp=mp, hc=hc: T.matmul(
                pst[0:1, SLAB:2 * SLAB], ones_bf[:],
                x4sq[:, mp * SLAB:(mp + 1) * SLAB],
                start=(hc == 0), stop=(hc == NPT - 1)), inc=True)
            x3bt_war[mp] = mm2
            x3sq_war[mp] = mm2
        if with_attnb:
            wrm = [("dg", rowdma3[-1])] if rowdma3 else []
            ln3_vc_prev, vc2 = emit_ln_rowmath(p, rows, 0, pst, mm2, epscol,
                                               extra_waits=wrm)
            p.dma("g", _ap2(stat3_dram, 0, 1, s * SLAB, SLAB, NTOK),
                  rows[0:1, 2 * SLAB:3 * SLAB], waits=[("mv", vc2)])
            rowdma3.append(p.dma("g", _ap2(stat3_dram, 1, 1, s * SLAB, SLAB, NTOK),
                                 rows[0:1, 3 * SLAB:4 * SLAB]))
            stats3_done.append(vc2)
    if not with_attnb:
        p.wait("q", [("dq", p.counts["dq"])])
        p.wait("g", [("dg", p.counts["dg"])])
        p.wait("s", [("ds", p.counts["ds"])])
        with es:
            p.realize(nc, es)
        return nc

    d_cw = None
    for wi, cw_in in enumerate((cwq_in, cwk_in, cwv_in)):
        d_cw = p.dma("g", bass.AP(cwall, wi * 4 * NPT,
                                  [[3 * 4 * NPT, 128], [4, NPT], [1, 4]]),
                     cw_in.rearrange("(c p) k -> p c k", p=128))
    d_onw = p.dma("g", onw[:],
                  bass.AP(onw_in.tensor, 0, [[1, 128], [128, 2]]),
                  allow_slow_non_contiguous=True)

    rep_war = [0, 0]
    xln3_war = 0
    wstr_war = [0, 0]
    zr_war = [0, 0, 0]
    silt_war = 0
    sqt_war = 0
    qn_read_t = 0
    qn_read_ds = 0
    pl2_war = 0
    prr_war = 0
    ktok_war = 0
    vtok_war = 0
    onT_war = 0
    o2b_war = 0
    x4bf_war = [0, 0]
    x4sq_war = [x3sq_war[0], x3sq_war[1]]
    x4_war = [(0, 0)] * 2
    x3t_war2 = [0, 0]
    seed3 = x3_war[:]
    ln4_vc_prev = ln3_vc_prev
    rowdma4 = 0
    mx_ms = 0
    rep4_used = [0, 0]
    lrow_war = 0
    lrbf_war = 0
    rdbg_dq = 0
    halo_wait = None
    dmp_dq = 0
    dmp2_dq = 0
    xdump_dq = 0
    qkvdump_dq = 0
    scan_dep = 0
    supd_sb = {0: 0, 1: 0, 2: 0, 3: 0}

    def fz(g):
        g.memset(Sf[:], 0.0)
        g.memset(halo[:], 0.0)
        return g.memset(Sb[:], 0.0)
    sz0 = p.op("g", fz, waits=[("mg", g0)], inc=True)
    state_ready = ("mg", sz0)

    for b in range(2):
        if b == 1:
            wfz = [("mv", scan_dep)] + [("ms", supd_sb[hh2]) for hh2 in range(4) if supd_sb[hh2]]
            sz1 = p.op("g", fz, waits=wfz, inc=True)
            state_ready = ("mg", sz1)
            halo_wait = ("mg", sz1)
        for sl in range(4):
            s = b * 4 + sl
            sp = s % 2
            w = [("dg", rowdma3[s]), ("mv", stats3_done[s])]
            if rep_war[sp]:
                w.append(("mv", rep_war[sp]))
            p.dma("g", rep[:, sp * 2 * SLAB: sp * 2 * SLAB + SLAB],
                  _bcast_row(stat3_dram, 0 * NTOK + s * SLAB, SLAB), waits=w)
            d_rep = p.dma("g", rep[:, sp * 2 * SLAB + SLAB:(sp + 1) * 2 * SLAB],
                          _bcast_row(stat3_dram, 1 * NTOK + s * SLAB, SLAB))
            vx3 = None
            for pt in range(NPT):
                tp = pt % 2
                wx = []
                if x3t_war2[tp]:
                    wx.append(("mv", x3t_war2[tp]))
                elif seed3 is not None:
                    dqw, msw = seed3[tp]
                    if dqw:
                        wx.append(dqw)
                    if msw:
                        wx.append(("ms", msw))
                d_x3 = p.dma("q", x3t[:, tp * SLAB:(tp + 1) * SLAB],
                             _ap2(x3_dram, pt * 128, 128, s * SLAB, SLAB, NTOK),
                             waits=wx)

                def fx(V, pt=pt, tp=tp, sp=sp):
                    V.tensor_tensor(xln3[:, pt * SLAB:(pt + 1) * SLAB],
                                    x3t[:, tp * SLAB:(tp + 1) * SLAB],
                                    rep[:, sp * 2 * SLAB: sp * 2 * SLAB + SLAB],
                                    ALU.mult)
                    return V.tensor_tensor(xln3[:, pt * SLAB:(pt + 1) * SLAB],
                                           xln3[:, pt * SLAB:(pt + 1) * SLAB],
                                           rep[:, sp * 2 * SLAB + SLAB:(sp + 1) * 2 * SLAB],
                                           ALU.subtract)
                wv = [("dq", d_x3), ("dg", d_rep)]
                if pt == 0 and xdump_dq:
                    wv.append(("dq", xdump_dq))
                    xdump_dq = 0
                if pt == 0 and xln3_war:
                    wv.append(("mt", xln3_war))
                vx3 = p.op("v", fx, waits=wv, inc=True)
                x3t_war2[tp] = vx3
            seed3 = None
            rep_war[sp] = vx3
            if debug and s == 7:
                xdump_dq = p.dma("q", x3dbg, xln3[:], waits=[("mv", vx3)])
            l2mm = {}
            mmq = None
            for wi, (w_in, dstn) in enumerate(((wq_in, qn), (wk_in, kn), (wv_in, vn))):
                for dc in range(NPT):
                    wp = (wi * NPT + dc) % 2
                    wld = []
                    if wstr_war[wp]:
                        wld.append(("mt", wstr_war[wp]))
                    d_w = p.dma("g", bass.AP(wstr, wp * NPT * 128,
                                             [[2 * NPT * 128, 128], [128, NPT], [1, 128]]),
                                bass.AP(w_in.tensor, dc * 128,
                                        [[C, 128], [128 * C, NPT], [1, 128]]),
                                waits=wld)
                    for kt in range(NPT):
                        wmm = []
                        if kt == 0:
                            wmm = [("dg", d_w), ("mv", vx3)]
                            if pd_free[wp]:
                                wmm.append(pd_free[wp])
                        mmq = p.op("t", lambda T, kt=kt, wp=wp: T.matmul(
                            pd[:, wp * SLAB:(wp + 1) * SLAB],
                            wstr[:, wp * NPT * 128 + kt * 128: wp * NPT * 128 + (kt + 1) * 128],
                            xln3[:, kt * SLAB:(kt + 1) * SLAB],
                            start=(kt == 0), stop=(kt == NPT - 1)),
                            waits=wmm, inc=(kt == NPT - 1))
                    wstr_war[wp] = mmq
                    base = wi * (SLAB + 4)
                    wz = [("mt", mmq)]
                    if halo_wait:
                        wz.append(halo_wait)
                        halo_wait = None
                    if zr_war[wi]:
                        wz.append(("mv", zr_war[wi]))
                    p.op("s", lambda S, wi=wi, dc=dc, base=base: S.copy(
                        zraw[:, base + 1: base + 4],
                        halo[:, (wi * NPT + dc) * 4: (wi * NPT + dc) * 4 + 3]),
                        waits=wz)
                    sev = p.op("s", lambda S, wp=wp, base=base: S.copy(
                        zraw[:, base + 4: base + 4 + SLAB],
                        pd[:, wp * SLAB:(wp + 1) * SLAB]), inc=True)
                    pd_free[wp] = ("ms", sev)

                    def fconv(V, wi=wi, dc=dc, base=base):
                        tap = lambda i: cwall[:, wi * 4 * NPT + dc * 4 + i: wi * 4 * NPT + dc * 4 + i + 1]
                        V.tensor_single_scalar(cvt[:], zraw[:, base + 1: base + 1 + SLAB],
                                               tap(0), ALU.mult)
                        V.scalar_tensor_tensor(cvt[:], zraw[:, base + 2: base + 2 + SLAB],
                                               tap(1), cvt[:], ALU.mult, ALU.add)
                        V.scalar_tensor_tensor(cvt[:], zraw[:, base + 3: base + 3 + SLAB],
                                               tap(2), cvt[:], ALU.mult, ALU.add)
                        return V.scalar_tensor_tensor(cvt[:], zraw[:, base + 4: base + 4 + SLAB],
                                                      tap(3), cvt[:], ALU.mult, ALU.add)
                    wcv = [("ms", sev), ("dg", d_cw)]
                    if silt_war:
                        wcv.append(("ms", silt_war))
                    vcv = p.op("v", fconv, waits=wcv, inc=True)
                    if debug and s == 0 and wi == 0 and dc == 0:
                        p.dma("q", zdbg, zraw[:], waits=[("ms", sev)])
                        p.dma("q", cvdbg, cvt[:], waits=[("mv", vcv)])
                    p.op("s", lambda S, wi=wi, dc=dc, base=base: S.copy(
                        halo[:, (wi * NPT + dc) * 4: (wi * NPT + dc) * 4 + 3],
                        zraw[:, base + 1 + SLAB: base + 4 + SLAB]),
                        waits=[("mv", vcv)], inc=True)
                    zr_war[wi] = vcv
                    wsl = [("mv", vcv)]
                    if sqt_war:
                        wsl.append(("mt", sqt_war))
                        sqt_war = 0
                    ssl = p.op("s", lambda S: S.activation(
                        silt[:], cvt[:], AF.Silu), waits=wsl, inc=True)
                    silt_war = ssl
                    wqn = [("ms", ssl)]
                    if qkvdump_dq:
                        wqn.append(("dq", qkvdump_dq))
                        qkvdump_dq = 0
                    if qn_read_t:
                        wqn.append(("mt", qn_read_t))
                        wqn.append(("ds", qn_read_ds))
                        qn_read_t = qn_read_ds = 0
                    scst = p.op("s", lambda S, dstn=dstn, dc=dc: S.copy(
                        dstn[:, dc * SLAB:(dc + 1) * SLAB], silt[:]), waits=wqn, inc=True)
                    if wi < 2:
                        ssq2 = p.op("s", lambda S: S.activation(
                            sqt[:], silt[:], AF.Square), inc=True)
                        hh = dc // 2
                        preg = hh % 2
                        wpm = [("ms", ssq2)]
                        if dc % 2 == 0:
                            if pl2_war:
                                wpm.append(("mv", pl2_war))
                            if ln4_vc_prev:
                                wpm.append(("mv", ln4_vc_prev))
                        mms = p.op("t", lambda T, preg=preg, dc=dc: T.matmul(
                            pst[0:1, preg * SLAB:(preg + 1) * SLAB],
                            ones_bf[:], sqt[:],
                            start=(dc % 2 == 0), stop=(dc % 2 == 1)),
                            waits=wpm, inc=True)
                        sqt_war = mms
                        if dc % 2 == 1:
                            wlr = [("mt", mms)]
                            if lrow_war:
                                wlr.append(("mv", lrow_war))
                            if rdbg_dq:
                                wlr.append(("dq", rdbg_dq))
                                rdbg_dq = 0
                            slr = p.op("s", lambda S, preg=preg: S.activation(
                                lrow[0:1, 0:SLAB], pst[0:1, preg * SLAB:(preg + 1) * SLAB],
                                AF.Sqrt, bias=epscol[0:1, 1:2]), waits=wlr, inc=True)
                            vlr = p.op("v", lambda V: V.reciprocal(
                                lrow[0:1, SLAB:2 * SLAB], lrow[0:1, 0:SLAB]),
                                waits=[("ms", slr)], inc=True)
                            pl2_war = vlr
                            lrow_war = vlr
                            wlb = [("mv", vlr)]
                            if lrbf_war:
                                wlb.append(("mt", lrbf_war))
                            slb = p.op("s", lambda S: S.copy(
                                lrbf[0:1, 0:SLAB], lrow[0:1, SLAB:2 * SLAB]),
                                waits=wlb, inc=True)
                            if debug and s == 0 and wi == 1 and hh == 0:
                                rdbg_dq = p.dma("q", rdbg, lrow[0:1, 0:2 * SLAB],
                                                waits=[("ms", slb)])
                            wrr = [("ms", slb)]
                            if prr_war:
                                wrr.append(("mv", prr_war))
                            mmr = None
                            for ch in range(4):
                                mmr = p.op("t", lambda T, ch=ch: T.matmul(
                                    pscan[:, ch * 128:(ch + 1) * 128], onesf[:],
                                    lrbf[0:1, ch * 128:(ch + 1) * 128],
                                    start=True, stop=True),
                                    waits=(wrr if ch == 0 else []), inc=(ch == 3))
                            lrbf_war = mmr
                            dstn2 = qn if wi == 0 else kn
                            vsc2 = None
                            for pp in range(2):
                                dc2 = hh * 2 + pp
                                if wi == 0:
                                    fsc = lambda V, dstn2=dstn2, dc2=dc2: V.scalar_tensor_tensor(
                                        dstn2[:, dc2 * SLAB:(dc2 + 1) * SLAB],
                                        dstn2[:, dc2 * SLAB:(dc2 + 1) * SLAB],
                                        0.0625, pscan[:, 0:SLAB], ALU.mult, ALU.mult)
                                else:
                                    fsc = lambda V, dstn2=dstn2, dc2=dc2: V.tensor_tensor(
                                        dstn2[:, dc2 * SLAB:(dc2 + 1) * SLAB],
                                        dstn2[:, dc2 * SLAB:(dc2 + 1) * SLAB],
                                        pscan[:, 0:SLAB], ALU.mult)
                                vsc2 = p.op("v", fsc,
                                    waits=([("mt", mmr)] if pp == 0 else []),
                                    inc=(pp == 1))
                            prr_war = vsc2
            xln3_war = mmq
            scale_done = prr_war
            if debug and s == 0:
                dw = [("mv", scale_done), ("ms", p.counts["ms"]), ("mv", p.counts["mv"])]
                p.dma("q", qn_dbg, qn[:], waits=dw)
                p.dma("q", kn_dbg, kn[:], waits=dw)
                qkvdump_dq = p.dma("q", vn_dbg, vn[:], waits=dw)
            son = None
            d_v1 = None
            mS = None
            for tc in range(4):
                tco = tc * 128
                for hh in range(4):
                    wkt = [("mv", scale_done)]
                    if ktok_war:
                        wkt.append(("mt", ktok_war))
                    if vtok_war:
                        wkt.append(("mv", vtok_war))
                    d_k0 = p.dmat("s", ktok[:, 0:128],
                                  kn[:, (2 * hh) * SLAB + tco: (2 * hh) * SLAB + tco + 128],
                                  waits=wkt)
                    p.dmat("s", ktok[:, 128:256],
                           kn[:, (2 * hh + 1) * SLAB + tco: (2 * hh + 1) * SLAB + tco + 128])
                    p.dmat("s", vtok[:, 0:128],
                           vn[:, (2 * hh) * SLAB + tco: (2 * hh) * SLAB + tco + 128])
                    d_v1 = p.dmat("s", vtok[:, 128:256],
                                  vn[:, (2 * hh + 1) * SLAB + tco: (2 * hh + 1) * SLAB + tco + 128])
                    qf0 = qn[:, (2 * hh) * SLAB + tco: (2 * hh) * SLAB + tco + 128]
                    qf1 = qn[:, (2 * hh + 1) * SLAB + tco: (2 * hh + 1) * SLAB + tco + 128]
                    kf0 = kn[:, (2 * hh) * SLAB + tco: (2 * hh) * SLAB + tco + 128]
                    kf1 = kn[:, (2 * hh + 1) * SLAB + tco: (2 * hh + 1) * SLAB + tco + 128]
                    wA = [("mv", scale_done), state_ready]
                    if dmp_dq:
                        wA.append(("dq", dmp_dq))
                        dmp_dq = 0
                    if dmp2_dq:
                        wA.append(("dq", dmp2_dq))
                        dmp2_dq = 0
                    if scan_dep:
                        wA.append(("mv", scan_dep))
                    if supd_sb[hh]:
                        wA.append(("ms", supd_sb[hh]))
                    p.op("t", lambda T, kf0=kf0: T.matmul(
                        pscan[:, 0:128], kf0, kf0, start=True, stop=False), waits=wA)
                    mA = p.op("t", lambda T, kf1=kf1: T.matmul(
                        pscan[:, 0:128], kf1, kf1, start=False, stop=True), inc=True)
                    vB = p.op("v", lambda V: V.tensor_tensor(
                        Bc[:, 0:128], pscan[:, 0:128], mlow[:], ALU.mult),
                        waits=[("mt", mA), ("dg", msk_dg)], inc=True)
                    vBT = p.op("v", lambda V: V.tensor_tensor(
                        Bc[:, 128:256], pscan[:, 0:128], mupn[:], ALU.mult),
                        inc=True)
                    if debug and s == 0 and tc == 0 and hh == 0:
                        p.dma("q", _ap2(msk_dbg, 0, 128, 0, 128, 384), mlow[:], waits=[("mv", vBT)])
                        p.dma("q", _ap2(msk_dbg, 0, 128, 128, 128, 384), mupn[:], waits=[("mv", vBT)])
                        p.dma("q", _ap2(msk_dbg, 0, 128, 256, 128, 384), mupi[:], waits=[("mv", vBT)])
                        dmp2_dq = p.dma("q", _ap2(bev_dbg, 0, 128, 0, 256, 7 * 256),
                                        Bc[:, 0:256], waits=[("mv", vBT)])
                    p.op("t", lambda T, kf0=kf0, hh=hh: T.matmul(
                        pB[:, 0:256], kf0, Sb[:, hh * 512: hh * 512 + 256],
                        start=True, stop=False), waits=[("mv", vBT)])
                    mR = p.op("t", lambda T, kf1=kf1, hh=hh: T.matmul(
                        pB[:, 0:256], kf1, Sb[:, hh * 512 + 256: hh * 512 + 512],
                        start=False, stop=True), inc=True)
                    vR = p.op("v", lambda V: V.tensor_tensor(
                        RHSb[:], vtok[:, 0:256], pB[:, 0:256], ALU.subtract),
                        waits=[("mt", mR), ("ds", d_v1)], inc=True)
                    xsrc = (RHSb, 0)
                    xpar = 0
                    bpar = 0
                    vX = vR
                    sB2 = 0
                    for lev in range(7):
                        bofs = bpar * 256
                        wap = [("mv", vX)]
                        if lev == 0:
                            wap.append(("mv", vBT))
                        if sB2:
                            wap.append(("ms", sB2))
                        mapp = p.op("t", lambda T, bofs=bofs, xs=xsrc: T.matmul(
                            pB[:, 256:512], Bc[:, bofs + 128: bofs + 256],
                            xs[0][:, xs[1]: xs[1] + 256],
                            start=True, stop=True), waits=wap, inc=True)
                        nxt = (Xb, xpar * 256)
                        vX = p.op("v", lambda V, nxt=nxt, xs=xsrc: V.tensor_tensor(
                            Xb[:, nxt[1]: nxt[1] + 256],
                            xs[0][:, xs[1]: xs[1] + 256],
                            pB[:, 256:512], ALU.add),
                            waits=[("mt", mapp)], inc=True)
                        xsrc = nxt
                        xpar ^= 1
                        if lev < 6:
                            nb = bpar ^ 1
                            wm1 = []
                            if dmp2_dq:
                                wm1.append(("dq", dmp2_dq))
                                dmp2_dq = 0
                            if sB2:
                                wm1.append(("ms", sB2))
                            m1 = None
                            if lev < 5:
                                m1 = p.op("t", lambda T, bofs=bofs: T.matmul(
                                    pscan[:, 128:256], Bc[:, bofs + 128: bofs + 256],
                                    Bc[:, bofs: bofs + 128], start=True, stop=True),
                                    waits=wm1, inc=True)
                            m2 = p.op("t", lambda T, bofs=bofs: T.matmul(
                                pscan[:, 256:384], Bc[:, bofs: bofs + 128],
                                Bc[:, bofs + 128: bofs + 256], start=True, stop=True),
                                waits=(wm1 if m1 is None else []), inc=True)
                            if lev < 5:
                                p.op("s", lambda S, nb=nb: S.copy(
                                    Bc[:, nb * 256: nb * 256 + 128], pscan[:, 128:256]),
                                    waits=[("mt", m1)])
                            sB2 = p.op("s", lambda S, nb=nb: S.copy(
                                Bc[:, nb * 256 + 128: nb * 256 + 256], pscan[:, 256:384]),
                                waits=[("mt", m2)], inc=True)
                            if debug and s == 0 and tc == 0 and hh == 0:
                                dmp2_dq = p.dma("q", _ap2(bev_dbg, 0, 128, (lev + 1) * 256, 256, 7 * 256),
                                                Bc[:, nb * 256: nb * 256 + 256], waits=[("ms", sB2)])
                            bpar = nb
                    Ut, Uo = xsrc
                    p.op("t", lambda T, kf0=kf0, qf0=qf0: T.matmul(
                        pscan[:, 384:512], kf0, qf0, start=True, stop=False))
                    mP = p.op("t", lambda T, kf1=kf1, qf1=qf1: T.matmul(
                        pscan[:, 384:512], kf1, qf1, start=False, stop=True), inc=True)
                    vP = p.op("v", lambda V: V.tensor_tensor(
                        Ptb[:], pscan[:, 384:512], mupi[:], ALU.mult),
                        waits=[("mt", mP)], inc=True)
                    for dvh in range(2):
                        for pp in range(2):
                            p.op("t", lambda T, dvh=dvh, pp=pp, hh=hh, qf0=qf0, qf1=qf1: T.matmul(
                                pC[:, dvh * 128:(dvh + 1) * 128],
                                Sb[:, hh * 512 + pp * 256 + dvh * 128: hh * 512 + pp * 256 + dvh * 128 + 128],
                                qf0 if pp == 0 else qf1,
                                start=(pp == 0), stop=False),
                                waits=([("mv", vP)] if (dvh == 0 and pp == 0) else []))
                        mO = p.op("t", lambda T, dvh=dvh, Ut=Ut, Uo=Uo: T.matmul(
                            pC[:, dvh * 128:(dvh + 1) * 128],
                            Ut[:, Uo + dvh * 128: Uo + dvh * 128 + 128],
                            Ptb[:], start=False, stop=True), inc=(dvh == 1))
                    wsu = [("mv", vX), ("ds", d_v1)]
                    p.op("t", lambda T, Ut=Ut, Uo=Uo: T.matmul(
                        pB[:, 0:256], ktok[:, 0:128], Ut[:, Uo: Uo + 256],
                        start=True, stop=True), waits=wsu)
                    mS = p.op("t", lambda T, Ut=Ut, Uo=Uo: T.matmul(
                        pB[:, 256:512], ktok[:, 128:256], Ut[:, Uo: Uo + 256],
                        start=True, stop=True), inc=True)

                    def fS(V, hh=hh):
                        V.tensor_tensor(Sf[:, hh * 512: hh * 512 + 256],
                                        Sf[:, hh * 512: hh * 512 + 256],
                                        pB[:, 0:256], ALU.add)
                        return V.tensor_tensor(Sf[:, hh * 512 + 256: hh * 512 + 512],
                                               Sf[:, hh * 512 + 256: hh * 512 + 512],
                                               pB[:, 256:512], ALU.add)
                    vS = p.op("v", fS, waits=[("mt", mS)], inc=True)
                    sSb = p.op("s", lambda S, hh=hh: S.copy(
                        Sb[:, hh * 512:(hh + 1) * 512], Sf[:, hh * 512:(hh + 1) * 512]),
                        waits=[("mv", vS)], inc=True)
                    scan_dep = vS
                    supd_sb[hh] = sSb
                    wo2 = [("mt", mO)]
                    if o2b_war:
                        wo2.append(("mv", o2b_war))
                    so2 = p.op("s", lambda S: S.activation(
                        o2b[:], pC[:, 0:256], AF.Square), waits=wo2, inc=True)
                    p.op("t", lambda T: T.matmul(
                        pC[0:1, 256:384], ones_bf[:], o2b[:, 0:128],
                        start=True, stop=False), waits=[("ms", so2)])
                    mss = p.op("t", lambda T: T.matmul(
                        pC[0:1, 256:384], ones_bf[:], o2b[:, 128:256],
                        start=False, stop=True), inc=True)
                    wsr = [("mt", mss)]
                    if lrow_war:
                        wsr.append(("mv", lrow_war))
                    srt = p.op("s", lambda S: S.activation(
                        lrow[0:1, 0:128], pC[0:1, 256:384],
                        AF.Sqrt, bias=epscol[0:1, 0:1], scale=1.0 / 256.0),
                        waits=wsr, inc=True)
                    vrc = p.op("v", lambda V: V.reciprocal(
                        lrow[0:1, 128:256], lrow[0:1, 0:128]),
                        waits=[("ms", srt)], inc=True)
                    lrow_war = vrc
                    scb = p.op("s", lambda S: S.copy(
                        rrb[0:1, 0:128], lrow[0:1, 128:256]),
                        waits=[("mv", vrc), ("mt", lrbf_war)], inc=True)
                    mrr = p.op("t", lambda T: T.matmul(
                        pC[:, 384:512], onesf[:], rrb[0:1, 0:128],
                        start=True, stop=True), waits=[("ms", scb)], inc=True)
                    lrbf_war = mrr
                    src2 = p.op("s", lambda S: S.copy(o2b[:, 0:128], pC[:, 384:512]),
                                waits=[("mt", mrr)], inc=True)
                    won = [("ms", src2), ("dg", d_onw)]
                    if onT_war:
                        won.append(("mt", onT_war))
                        onT_war = 0

                    def fon(V):
                        V.tensor_tensor(tmax[:, 0:128], pC[:, 0:128], o2b[:, 0:128], ALU.mult)
                        return V.tensor_tensor(tmax[:, 128:256], pC[:, 128:256], o2b[:, 0:128], ALU.mult)
                    von = p.op("v", fon, waits=won, inc=True)
                    o2b_war = von

                    def fon2(S, hh=hh, tco=tco):
                        S.activation(onT[:, (2 * hh) * SLAB + tco: (2 * hh) * SLAB + tco + 128],
                                     tmax[:, 0:128], AF.Identity, scale=onw[:, 0:1])
                        return S.activation(onT[:, (2 * hh + 1) * SLAB + tco: (2 * hh + 1) * SLAB + tco + 128],
                                            tmax[:, 128:256], AF.Identity, scale=onw[:, 1:2])
                    son = p.op("s", fon2, waits=[("mv", von)], inc=True)
                    if debug and s == 0 and tc in (0, 1) and hh == 0:
                        dst = sc_dbg if tc == 0 else sc2_dbg
                        dwx = [("ms", son)]
                        p.dma("q", _ap2(dst, 0, 128, 0, 512, 2048), Bc[:, 0:512], waits=dwx)
                        p.dma("q", _ap2(dst, 0, 128, 512, 256, 2048), RHSb[:], waits=dwx)
                        p.dma("q", _ap2(dst, 0, 128, 768, 512, 2048), Xb[:, 0:512], waits=dwx)
                        p.dma("q", _ap2(dst, 0, 128, 1280, 128, 2048), Ptb[:], waits=dwx)
                        p.dma("q", _ap2(dst, 0, 128, 1408, 256, 2048), o2b[:], waits=dwx)
                        dmp_dq = p.dma("q", _ap2(dst, 0, 1, 1664, 128, 2048), rrb[0:1, 0:128], waits=dwx)
                    if debug and s == 0 and tc == 0 and hh == 3:
                        dmp_dq = p.dma("q", sf_dbg, Sf[:], waits=[("ms", p.counts["ms"]), ("mv", p.counts["mv"])])
                    ktok_war = mS
                    vtok_war = vR
            qn_read_t = mS
            qn_read_ds = d_v1
            if debug and s == 0:
                p.dma("q", onT_dbg, onT[:], waits=[("ms", son)])
            mm4 = None
            for jc in range(NPT):
                mp = jc % 2
                wld = []
                if wstr_war[mp]:
                    wld.append(("mt", wstr_war[mp]))
                d_w = p.dma("g", bass.AP(wstr, mp * NPT * 128,
                                         [[2 * NPT * 128, 128], [128, NPT], [1, 128]]),
                            bass.AP(wo_in.tensor, jc * 128,
                                    [[C, 128], [128 * C, NPT], [1, 128]]),
                            waits=wld)
                mmo = None
                for kt in range(NPT):
                    wmm = []
                    if kt == 0:
                        wmm = [("dg", d_w), ("ms", son)]
                        if pd_free[mp]:
                            wmm.append(pd_free[mp])
                    mmo = p.op("t", lambda T, kt=kt, mp=mp: T.matmul(
                        pd[:, mp * SLAB:(mp + 1) * SLAB],
                        wstr[:, mp * NPT * 128 + kt * 128: mp * NPT * 128 + (kt + 1) * 128],
                        onT[:, kt * SLAB:(kt + 1) * SLAB],
                        start=(kt == 0), stop=(kt == NPT - 1)),
                        waits=wmm, inc=(kt == NPT - 1))
                wstr_war[mp] = mmo
                if jc == NPT - 1:
                    onT_war = mmo
                wxo = []
                if x3t_war2[mp]:
                    wxo.append(("mv", x3t_war2[mp]))
                d_x3 = p.dma("q", x3t[:, mp * SLAB:(mp + 1) * SLAB],
                             _ap2(x3_dram, jc * 128, 128, s * SLAB, SLAB, NTOK),
                             waits=wxo)
                wst = [("mt", mmo), ("dq", d_x3)]
                dqw, msw = x4_war[mp]
                if dqw:
                    wst.append(("dq", dqw))
                if msw:
                    wst.append(("ms", msw))
                vst4 = p.op("v", lambda V, mp=mp: V.tensor_tensor(
                    x4t[:, mp * SLAB:(mp + 1) * SLAB],
                    pd[:, mp * SLAB:(mp + 1) * SLAB],
                    x3t[:, mp * SLAB:(mp + 1) * SLAB], ALU.add),
                    waits=wst, inc=True)
                pd_free[mp] = ("mv", vst4)
                x3t_war2[mp] = vst4
                dq4 = p.dma("q", _ap2(x3b_out, jc * 128, 128, s * SLAB, SLAB, NTOK),
                            x4t[:, mp * SLAB:(mp + 1) * SLAB], waits=[("mv", vst4)])
                wbf = [("mv", vst4)]
                if jc < 2 and x4bf_war[sp]:
                    wbf.append(("mv", x4bf_war[sp]))
                sb4 = p.op("s", lambda S, jc=jc, mp=mp, sp=sp: S.copy(
                    x4bf[:, sp * NPT * SLAB + jc * SLAB: sp * NPT * SLAB + (jc + 1) * SLAB],
                    x4t[:, mp * SLAB:(mp + 1) * SLAB]), waits=wbf, inc=True)
                x4_war[mp] = (dq4, sb4)
                wsq = [("ms", sb4)]
                if x4sq_war[mp]:
                    wsq.append(("mt", x4sq_war[mp]))
                s24 = p.op("s", lambda S, jc=jc, mp=mp, sp=sp: S.activation(
                    x4sq[:, mp * SLAB:(mp + 1) * SLAB],
                    x4bf[:, sp * NPT * SLAB + jc * SLAB: sp * NPT * SLAB + (jc + 1) * SLAB],
                    AF.Square), waits=wsq, inc=True)
                wmm = [("ms", s24)]
                if jc == 0:
                    if ln4_vc_prev:
                        wmm.append(("mv", ln4_vc_prev))
                    if pl2_war:
                        wmm.append(("mv", pl2_war))
                p.op("t", lambda T, jc=jc, sp=sp: T.matmul(
                    pst[0:1, 0:SLAB], ones_bf[:],
                    x4bf[:, sp * NPT * SLAB + jc * SLAB: sp * NPT * SLAB + (jc + 1) * SLAB],
                    start=(jc == 0), stop=(jc == NPT - 1)), waits=wmm)
                mm4 = p.op("t", lambda T, mp=mp, jc=jc: T.matmul(
                    pst[0:1, SLAB:2 * SLAB], ones_bf[:],
                    x4sq[:, mp * SLAB:(mp + 1) * SLAB],
                    start=(jc == 0), stop=(jc == NPT - 1)), inc=True)
                x4sq_war[mp] = mm4
            wrm = [("dg", rowdma4)] if rowdma4 else ([("dg", rowdma3[-1])])
            ln4_vc_prev, vc4 = emit_ln_rowmath(p, rows, 0, pst, mm4, epscol,
                                               extra_waits=wrm)
            p.dma("q", _ap2(stat4_out, 0, 1, s * SLAB, SLAB, NTOK),
                  rows[0:1, 2 * SLAB:3 * SLAB], waits=[("mv", vc4)])
            p.dma("q", _ap2(stat4_out, 1, 1, s * SLAB, SLAB, NTOK),
                  rows[0:1, 3 * SLAB:4 * SLAB])
            p.dma("g", _ap2(stat4_dram, 0, 1, s * SLAB, SLAB, NTOK),
                  rows[0:1, 2 * SLAB:3 * SLAB], waits=[("mv", vc4)])
            rowdma4 = p.dma("g", _ap2(stat4_dram, 1, 1, s * SLAB, SLAB, NTOK),
                            rows[0:1, 3 * SLAB:4 * SLAB])
            wrep = [("dg", rowdma4)]
            if rep4_used[sp]:
                wrep.append(("mv", rep4_used[sp]))
            if s == 0:
                wrep.append(("mv", rep_war[0]))
                wrep.append(("mv", rep_war[1]))
            p.dma("g", rep[:, sp * 2 * SLAB: sp * 2 * SLAB + SLAB],
                  _bcast_row(stat4_dram, 0 * NTOK + s * SLAB, SLAB), waits=wrep)
            d_r4 = p.dma("g", rep[:, sp * 2 * SLAB + SLAB:(sp + 1) * 2 * SLAB],
                         _bcast_row(stat4_dram, 1 * NTOK + s * SLAB, SLAB))
            for jc in range(NPT):
                tp = jc % 2

                def fmx(V, sp=sp, jc=jc, tp=tp):
                    t1 = tmax[:, tp * SLAB:(tp + 1) * SLAB]
                    xb = x4bf[:, sp * NPT * SLAB + jc * SLAB: sp * NPT * SLAB + (jc + 1) * SLAB]
                    V.tensor_tensor(t1, xb, rep[:, sp * 2 * SLAB: sp * 2 * SLAB + SLAB], ALU.mult)
                    V.tensor_tensor(t1, t1, rep[:, sp * 2 * SLAB + SLAB:(sp + 1) * 2 * SLAB], ALU.subtract)
                    return V.reduce_max(mxt[:], t1, AX.X, apply_absolute_value=True)
                wmxx = [("dg", d_r4)] if jc == 0 else []
                if jc == 0 and son:
                    wmxx.append(("ms", son))
                red_ms = p.op("v", fmx, waits=wmxx, inc=True)
                mx_ms = p.op("v", lambda V: V.tensor_max(mxacc[:], mxacc[:], mxt[:]),
                             waits=[("mv", red_ms)], inc=True)
            x4bf_war[sp] = mx_ms
            rep4_used[sp] = mx_ms
    if with_attnb and debug:
        d_s3a = p.dma("g", Sf[0:8, 0:1024],
                      bass.AP(stat3_dram.tensor, 0, [[1024, 8], [1, 1024]]),
                      waits=[("mv", mx_ms), ("ms", son)])
        p.dma("g", st3_dbg, Sf[0:8, 0:1024], waits=[("dg", d_s3a)])
    emit_locmax_finish(p, bufs, mxacc, mxd_dram, lm_out, [("mv", mx_ms)])

    p.wait("q", [("dq", p.counts["dq"])])
    p.wait("g", [("dg", p.counts["dg"])])
    p.wait("s", [("ds", p.counts["ds"])])
    p.check()
    with es:
        p.realize(nc, es)
    return nc



def _mtri():
    i = np.arange(128)[:, None]
    f = np.arange(128)[None, :]
    m = np.zeros((128, 384), np.float32)
    m[:, 0:128] = np.where(i > f, -1.0, 0.0)
    m[:, 128:256] = np.where(i < f, -1.0, 0.0)
    m[:, 256:384] = np.where(i <= f, 1.0, 0.0)
    return m


def _shard_x(x_full):
    out = []
    for c in range(NCORES):
        xc = x_full[2 * c:2 * c + 2].reshape(2 * 2048, 1024)
        out.append(np.ascontiguousarray(xc.T))
    return out


def kernel(**inputs):
    x_full = np.asarray(inputs["L0_attn_a_x"], np.float32)
    xs = _shard_x(x_full)
    f32 = lambda k: np.ascontiguousarray(np.asarray(inputs[k], np.float32))
    row = lambda k: np.ascontiguousarray(np.asarray(inputs[k], np.float32).reshape(1, -1))
    clip_x = np.ascontiguousarray(np.asarray(inputs["L0_ffn_clip_x"], np.float32).reshape(1, 1))
    clip_d = np.ascontiguousarray(np.asarray(inputs["L0_ffn_clip_down"], np.float32).reshape(1, 1))
    w_gate = f32("L0_ffn_w_gate"); w_up = f32("L0_ffn_w_up"); w_down = f32("L0_ffn_w_down")
    mtri = _mtri()
    cores = list(range(NCORES))
    cat = lambda res, k: np.ascontiguousarray(np.concatenate(
        [np.asarray(res.results[c][k], np.float32).reshape(1, 1) for c in cores], axis=1))

    global _PROGS
    try:
        progs = _PROGS
    except NameError:
        progs = _PROGS = {}
    if "L1" not in progs:
        progs["L1"] = build_L1()
        progs["L2"] = build_L2()
        progs["L3"] = build_L3(True)
        progs["L5"] = build_L3(False)

    res1 = run_bass_kernel_spmd(progs["L1"], [{
        "x": xs[c], "qkv_w": f32("L0_attn_a_qkv_w"), "qkv_b": row("L0_attn_a_qkv_b"),
        "proj_w": f32("L0_attn_a_proj_w"), "proj_b": row("L0_attn_a_proj_b")}
        for c in cores], core_ids=cores)
    maxv1 = cat(res1, "locmax")

    res2 = run_bass_kernel_spmd(progs["L2"], [{
        "x2": np.asarray(res1.results[c]["x2"], np.float32),
        "stat": np.asarray(res1.results[c]["stat2"], np.float32),
        "maxv": maxv1, "clip": clip_x, "w_gate": w_gate, "w_up": w_up}
        for c in cores], core_ids=cores)
    maxv2 = cat(res2, "locmax")

    res3 = run_bass_kernel_spmd(progs["L3"], [{
        "h": np.asarray(res2.results[c]["h"], np.float32),
        "fvec": np.asarray(res2.results[c]["fvec"], np.float32),
        "maxv": maxv2, "clip": clip_d,
        "x2": np.asarray(res1.results[c]["x2"], np.float32),
        "w_down": w_down, "mtri": mtri,
        "Wq": f32("L0_attn_b_Wq"), "Wk": f32("L0_attn_b_Wk"),
        "Wv": f32("L0_attn_b_Wv"), "Wo": f32("L0_attn_b_Wo"),
        "conv_wq": f32("L0_attn_b_conv_wq"), "conv_wk": f32("L0_attn_b_conv_wk"),
        "conv_wv": f32("L0_attn_b_conv_wv"), "onorm_w": row("L0_attn_b_onorm_w")}
        for c in cores], core_ids=cores)
    maxv3 = cat(res3, "locmax")

    res4 = run_bass_kernel_spmd(progs["L2"], [{
        "x2": np.asarray(res3.results[c]["x3b"], np.float32),
        "stat": np.asarray(res3.results[c]["stat4"], np.float32),
        "maxv": maxv3, "clip": clip_x, "w_gate": w_gate, "w_up": w_up}
        for c in cores], core_ids=cores)
    maxv4 = cat(res4, "locmax")

    res5 = run_bass_kernel_spmd(progs["L5"], [{
        "h": np.asarray(res4.results[c]["h"], np.float32),
        "fvec": np.asarray(res4.results[c]["fvec"], np.float32),
        "maxv": maxv4, "clip": clip_d,
        "x2": np.asarray(res3.results[c]["x3b"], np.float32),
        "w_down": w_down}
        for c in cores], core_ids=cores)

    out = np.empty((16, 2048, 1024), np.float32)
    for c in cores:
        y = np.asarray(res5.results[c]["y"], np.float32)
        out[2 * c:2 * c + 2] = y.T.reshape(2, 2048, 1024)
    return out
